# version 01
# baseline (speedup 1.0000x reference)
import os
import numpy as np
import ml_dtypes
from contextlib import ExitStack

DBG_NO_DELTA = bool(int(os.environ.get("KDBG_NO_DELTA", "0")))
DBG_NO_BIAS = bool(int(os.environ.get("KDBG_NO_BIAS", "0")))
DBG_NO_PREP = bool(int(os.environ.get("KDBG_NO_PREP", "0")))

import concourse.bass as bass
import concourse.mybir as mybir
from concourse.bass_utils import run_bass_kernel_spmd

F32 = mybir.dt.float32
BF16 = mybir.dt.bfloat16
AF = mybir.ActivationFunctionType
ALU = mybir.AluOpType
AX = mybir.AxisListType
BF16NP = ml_dtypes.bfloat16

NCORES = 8
T = 1024
D = 4096
TT = T // 128
EPS = 1e-5

ABF_COLS = 81920
AF_COLS = 10240

ENGS = ("tensor", "vector", "scalar", "gpsimd", "sync")


class KB:
    def __init__(self):
        self.nc = bass.Bass("TRN2", target_bir_lowering=False)
        self.plan = {e: [] for e in ENGS}
        self.semv = {}
        self.sems = {}
        self.inputs = {}
        self.outputs = {}
        self.internal = {}
        self.ABF = self.nc.alloc_sbuf_tensor("ABF", [128, ABF_COLS], BF16)
        self.AF = self.nc.alloc_sbuf_tensor("AFA", [128, AF_COLS], F32)
        self.PS = [self.nc.alloc_psum_tensor(f"PS{i}", [128, 512], F32)
                   for i in range(8)]

    def inp(self, name, shape, dtype=F32):
        if name not in self.inputs:
            self.inputs[name] = self.nc.dram_tensor(
                name, list(shape), dtype, kind="ExternalInput").ap()
        return self.inputs[name]

    def out(self, name, shape, dtype=F32):
        if name not in self.outputs:
            self.outputs[name] = self.nc.dram_tensor(
                name, list(shape), dtype, kind="ExternalOutput").ap()
        return self.outputs[name]

    def dram(self, name, shape, dtype):
        if name not in self.internal:
            self.internal[name] = self.nc.dram_tensor(
                name, list(shape), dtype, kind="Internal").ap()
        return self.internal[name]

    def bf(self, off, n):
        assert 0 <= off and off + n <= ABF_COLS, (off, n)
        return self.ABF[:, off:off + n]

    def f32(self, off, n):
        assert 0 <= off and off + n <= AF_COLS, (off, n)
        return self.AF[:, off:off + n]

    def tok(self, sem):
        return (sem, self.semv.get(sem, 0))

    def step(self, eng, fn, waits=(), inc=None, n=1):
        if inc is not None:
            self.semv[inc] = self.semv.get(inc, 0) + n
        waits = tuple((s, v) for (s, v) in waits if s is not None and v > 0)

        def emitfn(e, fn=fn, waits=waits, inc=inc, n=n):
            for (s, v) in waits:
                e.wait_ge(self.sems[s], v)
            r = fn(e)
            if inc is not None:
                last = r[-1] if isinstance(r, (list, tuple)) else r
                last.then_inc(self.sems[inc], n)
        self.plan[eng].append(emitfn)
        return self.tok(inc) if inc is not None else None

    def dma(self, eng, out, in_, waits=(), sem="a", **kw):
        return self.step(eng, lambda e: e.dma_start(out=out, in_=in_, **kw),
                         waits=waits, inc=sem, n=16)

    def chain(self, eng, fns, waits=(), inc=None):
        sem = inc or {"vector": "ve", "scalar": "se", "gpsimd": "po",
                      "tensor": "pe"}[eng]
        tok = None
        for i, fn in enumerate(fns):
            w = list(waits) if i == 0 else [tok]
            tok = self.step(eng, fn, waits=w, inc=sem)
        return tok

    def ccol(self, val):
        if not hasattr(self, "_ccols"):
            self._ccols = {}
            self._ccol_next = AF_COLS - 64
        if val not in self._ccols:
            off = self._ccol_next
            self._ccol_next += 1
            ap = self.f32(off, 1)
            tok = self.step("vector", lambda e, ap=ap, val=val: e.memset(ap, val),
                            inc="ve")
            self._ccols[val] = (ap, tok)
        return self._ccols[val]

    def barrier_tokens(self):
        return [self.tok(s) for s in list(self.semv)]

    def debug_dump(self):
        reqs = getattr(self, "dbg_reqs", [])
        if not reqs:
            return
        allw = self.barrier_tokens()
        for i, (name, kind, src) in enumerate(reqs):
            if kind == "dram":
                ap = self.internal[src]
                o = self.out("dbg_" + name, list(ap.shape), ap.dtype)
                self.dma("gpsimd", out=o, in_=ap, waits=allw, sem="g")
            elif kind == "sbuf_f32":
                off, n = src
                ap = self.f32(off, n)
                o = self.out("dbg_" + name, [128, n], F32)
                self.dma("gpsimd", out=o, in_=ap, waits=allw, sem="g")
            elif kind == "sbuf_bf":
                off, n = src
                ap = self.bf(off, n)
                o = self.out("dbg_" + name, [128, n], BF16)
                self.dma("gpsimd", out=o, in_=ap, waits=allw, sem="g")

    def finalize(self):
        ctx = ExitStack()
        for name in self.semv:
            self.sems[name] = ctx.enter_context(self.nc.semaphore(name))
        fin = ctx.enter_context(self.nc.semaphore("fin"))
        used = [e for e in ENGS if self.plan[e]]
        n_other = len([e for e in used if e != "gpsimd"])
        with self.nc.Block() as block:
            for eng in ENGS:
                fns = self.plan[eng]
                if not fns and eng != "gpsimd":
                    continue

                def run_all(e, fns=fns, eng=eng):
                    for f in fns:
                        f(e)
                    if eng != "gpsimd":
                        e.nop().then_inc(fin, 1)
                    else:
                        e.wait_ge(fin, n_other)
                        for name, v in self.semv.items():
                            if v > 0:
                                e.wait_ge(self.sems[name], v)
                        nums = sorted(h.num for h in
                                      list(self.sems.values()) + [fin])
                        self.nc.gpsimd.dma_reset(range(min(nums), max(nums) + 1))
                        for h in list(self.sems.values()) + [fin]:
                            self.nc.gpsimd.sem_clear(h)
                getattr(block, eng)(run_all)
        self._ctx = ctx
        return self.nc


def host_constants():
    c = {}
    eye = np.eye(128, dtype=np.float32)
    c["c_id_bf"] = eye.astype(BF16NP)
    c["c_halfid_bf"] = (0.5 * eye).astype(BF16NP)
    trili = (np.arange(128)[:, None] <= np.arange(128)[None, :]).astype(np.float32)
    c["c_trili_bf"] = trili.astype(BF16NP)
    c["c_trili_f32"] = trili
    c["c_trius_bf"] = (np.arange(128)[:, None] >
                       np.arange(128)[None, :]).astype(BF16NP)
    c["c_ones_bf"] = np.ones((128, 128), BF16NP)
    c["c_negones_bf"] = np.full((128, 128), -1.0, BF16NP)
    c["c_irep"] = (np.arange(128)[:, None] % 32 ==
                   np.arange(32)[None, :]).astype(np.float32)
    c["c_stack_bf"] = (np.arange(32)[:, None] ==
                       (np.arange(128)[None, :] % 32)).astype(BF16NP)
    c["c_halfones_bf"] = np.full((1, 128), 0.5, BF16NP)
    return c


def host_prep(inputs):
    f = np.float32
    A = {k: np.asarray(v) for k, v in inputs.items()}
    sh = dict(host_constants())

    bflat = np.ascontiguousarray(
        A["L0_ffn_lora_B"].transpose(0, 2, 1).reshape(512, D)).astype(f)
    bsh = np.ascontiguousarray(A["L0_ffn_lora_B_shared"].T).astype(f)
    sh["w_ffn0"] = np.ascontiguousarray(np.concatenate(
        [A["L0_ffn_kernel"].astype(f), bflat, bsh, np.zeros((96, D), f)], axis=0))
    sh["l0f_biasr"] = np.ascontiguousarray(A["L0_ffn_bias"].astype(f)[None, :])
    sh["l0f_AflatT"] = np.ascontiguousarray(A["L0_ffn_lora_A"].reshape(512, D).T).astype(f)
    sh["l0f_ashT"] = np.ascontiguousarray(A["L0_ffn_lora_A_shared"].T).astype(f)

    wproj = np.ascontiguousarray(A["L0_attn_in_proj_w"].T).astype(f)
    sh["l0a_wproj"] = np.ascontiguousarray(
        np.concatenate([wproj, np.zeros((D, 18944 - 18560), f)], axis=1))
    sh["l0a_woT"] = np.ascontiguousarray(A["L0_attn_out_proj_w"].T).astype(f)
    sh["l0a_nw"] = np.ascontiguousarray(
        np.broadcast_to(A["L0_attn_norm_w"].astype(f)[None, :], (128, D)))
    sh["l0a_convw"] = np.ascontiguousarray(
        A["L0_attn_conv_w"].astype(f).reshape(80, 128, 4).transpose(1, 0, 2)
        .reshape(128, 320))
    sh["l0a_convb"] = np.ascontiguousarray(
        A["L0_attn_conv_b"].astype(f).reshape(80, 128).T)
    sh["l0a_dtb"] = np.ascontiguousarray(
        np.broadcast_to(A["L0_attn_dt_bias"].astype(f)[None, :], (128, 128)))
    sh["l0a_alog"] = np.ascontiguousarray(
        np.broadcast_to(A["L0_attn_A_log"].astype(f)[None, :], (128, 128)))
    sh["l0a_dp"] = np.ascontiguousarray(
        np.broadcast_to(np.repeat(A["L0_attn_Dp"].astype(f), PDIM)[None, :],
                        (128, DINNER)))
    sh["l0a_g0col"] = np.ascontiguousarray(
        A["L0_attn_gnorm_w"].astype(f).reshape(64, 128).T)

    wcat = np.concatenate([A["L1_attn_Wq"].T, A["L1_attn_Wk"].T,
                           A["L1_attn_Wv"].T, A["L1_attn_Wg"].T,
                           A["L1_attn_Win"].T,
                           np.zeros((D, 10752 - 10272), np.float32)],
                          axis=1).astype(f)
    sh["l1a_wcat"] = np.ascontiguousarray(wcat)
    sh["l1a_woT"] = np.ascontiguousarray(A["L1_attn_Wo"].T).astype(f)
    sh["l1a_dtb"] = np.ascontiguousarray(
        np.broadcast_to(A["L1_attn_dt_bias"].astype(f)[None, :], (128, 32)))
    sh["l1a_alog"] = np.ascontiguousarray(
        np.broadcast_to(A["L1_attn_A_log_bias"].astype(f)[None, :], (128, 32)))
    sh["l1a_g1"] = np.ascontiguousarray(
        np.broadcast_to(A["L1_attn_gnorm_w"].astype(f)[None, :], (128, 128)))
    inv_freq = 1.0 / 10000.0 ** (np.arange(0, 128, 2) / 128.0)

    sh["l1f_WqT"] = np.ascontiguousarray(A["L1_ffn_Wq_w"].T).astype(f)
    sh["l1f_keysT"] = np.ascontiguousarray(A["L1_ffn_keys"].T).astype(f)
    sh["l1f_AwT"] = np.ascontiguousarray(A["L1_ffn_A_w"].T).astype(f)
    sh["l1f_BwT"] = np.ascontiguousarray(A["L1_ffn_B_w"].T).astype(f)

    percore = [dict() for _ in range(NCORES)]
    x_full = np.ascontiguousarray(A["L0_attn_x"].reshape(8192, D).astype(f))
    gate = A["L0_ffn_gate"].astype(f)
    for k in range(NCORES):
        pc = percore[k]
        pc["xin"] = np.ascontiguousarray(x_full[k * T:(k + 1) * T])
        cm = np.zeros((1, 16), f)
        cm[0, :k] = 1.0
        cm[0, 8:16] = 1.0 - cm[0, 0:8]
        pc["l0a_cm"] = cm
        place = np.zeros((16, 8), f)
        if k < 7:
            place[:, k + 1] = 1.0
        pc["l0a_place"] = place
        ang = (np.arange(k * T, (k + 1) * T, dtype=np.float64)[:, None] *
               (1.0 / 10000.0 ** (np.arange(0, 128, 2) / 128.0))[None, :])
        pc["l1_cos"] = np.cos(ang).astype(f)
        pc["l1_sin"] = np.sin(ang).astype(f)
        pc["l0f_gpart"] = np.ascontiguousarray(
            np.stack([np.repeat(gate[k, 4 * m:4 * m + 4], 32)
                      for m in range(4)], axis=1))
    return sh, percore


HT_OFF = 0
WB_A = 32768
WB_COLS = 18944
WB_B = WB_A + WB_COLS
SC_OFF = WB_B + WB_COLS
LN_J0 = WB_A
LN_J1 = WB_A + 4096
LN_H0 = WB_A + 8192
LN_H1 = WB_A + 12288

AFX0, AFX1 = 0, 4096
AFST = 8192
AFSM = 8320


def emit_ln(kb, xin, *, scale_mul, waits, double_rms=False, nw_bf=None,
            h_dram=None, ht_kt=32, keep_ln1=None):
    XS = [kb.f32(AFX0, D), kb.f32(AFX1, D)]
    JS = [kb.bf(LN_J0, D), kb.bf(LN_J1, D)]
    HS = [kb.bf(LN_H0, D), kb.bf(LN_H1, D)]
    stc = lambda i: kb.f32(AFST + i, 1)
    act_toks = []
    store_toks = []
    sqv_toks = []
    for t in range(TT):
        xs = XS[t % 2]
        w = list(waits) if t < 2 else [act_toks[t - 2]]
        ld = kb.dma("sync", out=xs, in_=xin[t * 128:(t + 1) * 128, :],
                    waits=w, sem="a")
        base = 16 * (t % 2)
        s1, ssq, mcol, var, rs, nb, sd, q = [stc(base + i) for i in range(8)]
        tv0 = kb.step("vector", lambda e, xs=xs, s1=s1:
                      e.reduce_sum(s1, xs, axis=AX.X), waits=[ld], inc="ve")
        junk = JS[t % 2]
        wj = [ld]
        if len(sqv_toks) >= 2:
            wj.append(sqv_toks[t - 2])
        sq = kb.step("scalar", lambda e, xs=xs, junk=junk:
                     e.activation(junk, xs, AF.Square),
                     waits=wj, inc="se")
        sq = kb.step("vector", lambda e, ssq=ssq, junk=junk:
                     e.reduce_sum(ssq, junk, axis=AX.X),
                     waits=[sq], inc="ve")
        sqv_toks.append(sq)

        tv = kb.chain("vector", [
            lambda e, mcol=mcol, s1=s1: e.tensor_scalar_mul(mcol, s1, 1.0 / D),
            lambda e, var=var, ssq=ssq: e.tensor_scalar_mul(var, ssq, 1.0 / D),
            lambda e, nb=nb, mcol=mcol: e.scalar_tensor_tensor(
                nb, mcol, -1.0, mcol, ALU.mult, ALU.mult),
            lambda e, var=var, nb=nb: e.tensor_add(var, var, nb),
        ], waits=[sq, tv0])
        epsap, epstok = kb.ccol(EPS)
        tsd = kb.step("scalar", lambda e, var=var, sd=sd, epsap=epsap:
                      e.activation(sd, var, AF.Sqrt, bias=epsap),
                      waits=[tv, epstok], inc="se")
        tre = kb.step("vector", lambda e, sd=sd, rs=rs: e.reciprocal(rs, sd),
                      waits=[tsd], inc="ve")
        if keep_ln1 is not None:
            rs1 = kb.f32(keep_ln1 + 2 * t, 1)
            nb1 = kb.f32(keep_ln1 + 2 * t + 1, 1)
            tre = kb.chain("vector", [
                lambda e, rs1=rs1, rs=rs: e.tensor_copy(rs1, rs),
                lambda e, nb1=nb1, mcol=mcol, rs1=rs1: e.scalar_tensor_tensor(
                    nb1, mcol, -1.0, rs1, ALU.mult, ALU.mult),
            ], waits=[tre])
        if double_rms:
            t1 = kb.chain("vector", [
                lambda e, q=q, rs=rs: e.tensor_mul(q, rs, rs),
                lambda e, q=q, var=var: e.tensor_mul(q, q, var),
            ], waits=[tre])
            t2 = kb.step("scalar", lambda e, q=q, sd=sd, epsap=epsap:
                         e.activation(sd, q, AF.Sqrt, bias=epsap),
                         waits=[t1, epstok], inc="se")

            tre = kb.chain("vector", [
                lambda e, q=q, sd=sd: e.reciprocal(q, sd),
                lambda e, rs=rs, q=q: e.tensor_mul(rs, rs, q),
            ], waits=[t2])

        tf = kb.chain("vector", [
            lambda e, rs=rs: e.tensor_scalar_mul(rs, rs, scale_mul),
            lambda e, nb=nb, mcol=mcol, rs=rs: e.scalar_tensor_tensor(
                nb, mcol, -1.0, rs, ALU.mult, ALU.mult),
        ], waits=[tre])
        hs = HS[t % 2]
        w2 = [tf, ld]
        if len(store_toks) >= 2:
            w2.append(store_toks[t - 2])
        ta = kb.step("scalar", lambda e, xs=xs, hs=hs, rs=rs, nb=nb:
                     e.activation(hs, xs, AF.Identity, bias=nb, scale=rs),
                     waits=w2, inc="se")
        if nw_bf is not None:
            ta = kb.step("vector", lambda e, hs=hs, nwt=nw_bf:
                         e.tensor_mul(hs, hs, nwt), waits=[ta], inc="ve")
        act_toks.append(ta)
        st = kb.dma("sync", out=h_dram[t * 128:(t + 1) * 128, :], in_=hs,
                    waits=[ta], sem="st")
        store_toks.append(st)
    tr_tok = None
    for k in range(ht_kt):
        tr_tok = kb.dma("sync", out=kb.bf(HT_OFF + k * T, T),
                        in_=h_dram[:, k * 128:(k + 1) * 128],
                        waits=[store_toks[-1]] if k == 0 else (), sem="a",
                        transpose=True)
    return tr_tok


def emit_proj(kb, wname, KT_W, O, *, evac, bw=512, lhsT=None, pre_mm=None,
              mm_waits=(), load0_waits=(), load1_waits=(), wb_a=WB_A,
              wb_b=WB_B, psum_banks=(0, 1, 2, 3), wsem="w"):
    W = kb.inp(wname, [KT_W * 128, O])
    NB = (O + bw - 1) // bw
    assert O % bw == 0
    if lhsT is None:
        lhsT = lambda kt, tt: kb.bf(HT_OFF + kt * T + tt * 128, 128)
    WBs = (wb_a, wb_b)
    nbanks = len(psum_banks)
    load_toks = {}
    pe_toks = {}
    free_toks = {}

    def plan_load(b):
        wbo = WBs[b % 2]
        if b == 0:
            w = list(load0_waits)
        elif b == 1:
            w = list(load1_waits)
        else:
            w = [pe_toks[(b - 2, TT - 1)]]
        load_toks[b] = kb.dma(
            "gpsimd",
            out=kb.bf(wbo, KT_W * bw).rearrange("p (kt o) -> p kt o", kt=KT_W),
            in_=W[:, b * bw:(b + 1) * bw].rearrange("(kt p) o -> p kt o", p=128),
            waits=w, sem=wsem)

    plan_load(0)
    if NB > 1:
        plan_load(1)
    grp = 0
    for b in range(NB):
        wbo = WBs[b % 2]
        for tt in range(TT):
            bank = psum_banks[grp % nbanks]
            ps = kb.PS[bank][:, 0:bw]
            w = [load_toks[b]]
            if grp == 0:
                w += list(mm_waits)
            if grp >= nbanks:
                w.append(free_toks[grp - nbanks])

            def mms(e, b=b, tt=tt, ps=ps, wbo=wbo):
                r = []
                started = False
                if pre_mm is not None:
                    r += pre_mm(e, ps, b, tt)
                    started = True
                for kt in range(KT_W):
                    r.append(e.matmul(
                        ps, lhsT(kt, tt),
                        kb.bf(wbo + kt * bw, bw),
                        start=(not started and kt == 0), stop=(kt == KT_W - 1)))
                return r
            pe_toks[(b, tt)] = kb.step("tensor", mms, waits=w, inc="pe")
            free_toks[grp] = evac(b, tt, grp, bank, pe_toks[(b, tt)])
            grp += 1
        if b + 2 < NB:
            plan_load(b + 2)
    return pe_toks, free_toks


def emit_l0ffn(kb, xin_name, xout_name, x_ready, first=False):
    xin = kb.inp(xin_name, [T, D]) if first else kb.dram(xin_name, [T, D], F32)
    xout = (kb.out(xout_name, [T, D]) if xout_name == "xout"
            else kb.dram(xout_name, [T, D], F32))
    h_dram = kb.dram("h_dram", [T, D], BF16)

    hT_tok = emit_ln(kb, xin, scale_mul=0.5, waits=x_ready, h_dram=h_dram)

    AXG = SC_OFF
    EFFR = SC_OFF + 4096
    SHX = SC_OFF + 5120
    BIASR = SC_OFF + 6144
    HONES = SC_OFF + 10240
    assert HONES + 128 <= ABF_COLS
    AFL = WB_B
    ASH = WB_B + 16384
    STKO = WB_B + 17408

    bias_in = kb.inp("l0f_biasr", [1, D])
    gp_in = kb.inp("l0f_gpart", [128, 4])
    irep_in = kb.inp("c_irep", [128, 32])
    stack_in = kb.inp("c_stack_bf", [32, 128], BF16)
    hones_in = kb.inp("c_halfones_bf", [1, 128], BF16)
    afl_in = kb.inp("l0f_AflatT", [D, 512])
    ash_in = kb.inp("l0f_ashT", [D, 32])

    BIASRT = kb.ABF[0:1, BIASR:BIASR + D]
    HONT = kb.ABF[0:1, HONES:HONES + 128]
    STK = kb.ABF[0:32, STKO:STKO + 128]
    ld_bias = kb.dma("gpsimd", out=BIASRT, in_=bias_in, sem="g")
    ld_hon = kb.dma("gpsimd", out=HONT, in_=hones_in, sem="g")
    ld_stack = kb.dma("gpsimd", out=STK, in_=stack_in, sem="g")
    ld_afl = kb.dma("gpsimd",
                    out=kb.bf(AFL, 16384).rearrange("p (kt o) -> p kt o", kt=32),
                    in_=afl_in.rearrange("(kt p) o -> p kt o", p=128), sem="g")
    ld_ash = kb.dma("gpsimd",
                    out=kb.bf(ASH, 1024).rearrange("p (kt o) -> p kt o", kt=32),
                    in_=ash_in.rearrange("(kt p) o -> p kt o", p=128), sem="g")
    GP = kb.f32(AFSM, 4)
    IREP = kb.f32(AFSM + 8, 32)
    ld_gp = kb.dma("sync", out=GP, in_=gp_in, sem="a")
    ld_ir = kb.dma("sync", out=IREP, in_=irep_in, sem="a")

    GT = [kb.bf(81024 + 32 * m, 32) for m in range(4)]

    def build_gt(e):
        return [e.tensor_scalar_mul(GT[m], IREP, GP[:, m:m + 1])
                for m in range(4)]
    t_gt = kb.step("vector", build_gt, waits=[ld_gp, ld_ir], inc="ve")

    axt_toks = []
    for m in range(4):
        for hh in range(2):
            bank = 4 + (m * 2 + hh) % 2
            ps = kb.PS[bank]
            w = [ld_afl, hT_tok]
            if len(axt_toks) >= 2:
                w.append(axt_toks[-2])

            def mm(e, m=m, hh=hh, ps=ps):
                r = []
                for kt in range(32):
                    r.append(e.matmul(
                        ps[:, :], kb.bf(AFL + kt * 512 + m * 128, 128),
                        kb.bf(HT_OFF + kt * T + hh * 512, 512),
                        start=(kt == 0), stop=(kt == 31)))
                return r
            tm = kb.step("tensor", mm, waits=w, inc="pe")
            te = kb.step("vector", lambda e, m=m, hh=hh, ps=ps:
                         e.tensor_copy(kb.bf(AXG + m * 1024 + hh * 512, 512),
                                       ps[:, :]),
                         waits=[tm], inc="ve")
            axt_toks.append(te)
    eff_toks = []
    for hh in range(2):
        ps6 = kb.PS[6]

        def mm(e, hh=hh, ps=ps6):
            r = []
            for m in range(4):
                r.append(e.matmul(ps[0:32, :], GT[m],
                                  kb.bf(AXG + m * 1024 + hh * 512, 512),
                                  start=(m == 0), stop=(m == 3)))
            return r
        tm = kb.step("tensor", mm, waits=axt_toks + [t_gt] + eff_toks, inc="pe")
        te = kb.step("vector", lambda e, hh=hh, ps=ps6:
                     e.tensor_copy(kb.ABF[0:32, EFFR + hh * 512:EFFR + hh * 512 + 512],
                                   ps[0:32, :]),
                     waits=[tm], inc="ve")
        eff_toks.append(te)
    shx_zero = kb.step("vector", lambda e: e.memset(kb.bf(SHX, 1024), 0.0),
                       inc="ve")
    shx_toks = []
    for hh in range(2):
        ps7 = kb.PS[7]

        def mm(e, hh=hh, ps=ps7):
            r = []
            for kt in range(32):
                r.append(e.matmul(ps[0:32, :], kb.bf(ASH + kt * 32, 32),
                                  kb.bf(HT_OFF + kt * T + hh * 512, 512),
                                  start=(kt == 0), stop=(kt == 31)))
            return r
        tm = kb.step("tensor", mm, waits=[ld_ash, hT_tok] + shx_toks, inc="pe")
        te = kb.step("vector", lambda e, hh=hh, ps=ps7:
                     e.tensor_copy(kb.ABF[0:32, SHX + hh * 512:SHX + hh * 512 + 512],
                                   ps[0:32, :]),
                     waits=[tm, shx_zero], inc="ve")
        shx_toks.append(te)
    axg_toks = []
    for j in range(4):
        for hh in range(2):
            bank = 4 + (j * 2 + hh) % 2
            ps = kb.PS[bank]
            w = eff_toks + [ld_stack]
            if len(axg_toks) >= 2:
                w.append(axg_toks[-2])

            def mm(e, hh=hh, ps=ps):
                return e.matmul(ps[:, :], STK,
                                kb.ABF[0:32, EFFR + hh * 512:EFFR + hh * 512 + 512],
                                start=True, stop=True)
            tm = kb.step("tensor", mm, waits=w, inc="pe")
            te = kb.step("vector", lambda e, j=j, hh=hh, ps=ps:
                         e.tensor_scalar_mul(
                             kb.bf(AXG + j * 1024 + hh * 512, 512), ps[:, :],
                             GP[:, j:j + 1]),
                         waits=[tm], inc="ve")
            axg_toks.append(te)
    prep_done = axg_toks + shx_toks + [ld_bias, ld_hon]

    def lhsT(kt, tt):
        if kt < 32:
            return kb.bf(HT_OFF + kt * T + tt * 128, 128)
        if kt < 36:
            return kb.bf(AXG + (kt - 32) * 1024 + tt * 128, 128)
        return kb.bf(SHX + tt * 128, 128)

    def pre_mm(e, ps, b, tt):
        return [e.matmul(ps, HONT, kb.ABF[0:1, BIASR + b * 512:BIASR + b * 512 + 512],
                         start=True, stop=False)]

    ep_x = [kb.f32(0 + i * 512, 512) for i in range(4)]
    ep_o = [kb.f32(4096 + i * 512, 512) for i in range(4)]
    st_toks = {}
    ld_toks = {}

    def evac(b, tt, grp, bank, pe_tok):
        slot = grp % 4
        w = []
        if grp >= 4:
            w.append(st_toks[grp - 4])
        ld_toks[grp] = kb.dma(
            "sync", out=ep_x[slot],
            in_=xin[tt * 128:(tt + 1) * 128, b * 512:(b + 1) * 512],
            waits=w, sem="a")
        tv = kb.step("vector", lambda e, slot=slot, bank=bank:
                     e.tensor_add(ep_o[slot], ep_x[slot], kb.PS[bank][:, 0:512]),
                     waits=[pe_tok, ld_toks[grp]], inc="ve")
        st_toks[grp] = kb.dma(
            "sync", out=xout[tt * 128:(tt + 1) * 128, b * 512:(b + 1) * 512],
            in_=ep_o[slot], waits=[tv], sem="st")
        return tv

    emit_proj(kb, "w_ffn0", 32 if DBG_NO_DELTA else 37, D, evac=evac,
              pre_mm=None if DBG_NO_BIAS else pre_mm, lhsT=lhsT,
              mm_waits=prep_done,
              load0_waits=[hT_tok],
              load1_waits=[axg_toks[-1], shx_toks[-1], axt_toks[-1]])
    return kb.tok("st")


DINNER = 8192
CONVD = 10240
PROJD = 18560
NHEAD = 128
PDIM = 64
NG = 8
NDIM = 128
NCH = 8


def emit_l0attn(kb, xin_name, xout_name, x_ready, first=False):
    xin = kb.inp(xin_name, [T, D]) if first else kb.dram(xin_name, [T, D], F32)
    xout = (kb.out(xout_name, [T, D]) if xout_name == "xout"
            else kb.dram(xout_name, [T, D], F32))
    h_dram = kb.dram("h_dram", [T, D], BF16)
    sg_dram = kb.dram("sg_dram", [T, DINNER], BF16)
    hbc_dram = kb.dram("hbc_dram", [T, CONVD], BF16)
    co_dram = kb.dram("co_dram", [CONVD, T], BF16)
    y_dram = kb.dram("y_dram", [T, DINNER], BF16)

    WA, WBB = 32768, 49152
    NWO = 65536
    STG = 69632
    LN1 = AFSM + 16

    nw_in = kb.inp("l0a_nw", [128, D])
    ld_nw = kb.dma("gpsimd", out=kb.bf(NWO, D), in_=nw_in, sem="g")
    hT_tok = emit_ln(kb, xin, scale_mul=1.0, waits=list(x_ready) + [ld_nw],
                     double_rms=True, nw_bf=kb.bf(NWO, D), h_dram=h_dram,
                     keep_ln1=LN1)

    st_sg = {}
    dt_toks = []

    def evac_ip(b, tt, grp, bank, pe_tok):
        slot = grp % 8
        stg = kb.bf(STG + slot * 512, 512)
        w = [pe_tok]
        if grp >= 8:
            w.append(st_sg[grp - 8])
        if b < 16:
            tv = kb.step("scalar", lambda e, stg=stg, bank=bank:
                         e.activation(stg, kb.PS[bank][:, 0:512], AF.Silu),
                         waits=w, inc="se")
            st_sg[grp] = kb.dma(
                "sync", out=sg_dram[tt * 128:(tt + 1) * 128,
                                    b * 512:(b + 1) * 512],
                in_=stg, waits=[tv], sem="st")
        elif b < 36:
            tv = kb.step("vector", lambda e, stg=stg, bank=bank:
                         e.tensor_copy(stg, kb.PS[bank][:, 0:512]),
                         waits=w, inc="ve")
            st_sg[grp] = kb.dma(
                "sync", out=hbc_dram[tt * 128:(tt + 1) * 128,
                                     (b - 16) * 512:(b - 15) * 512],
                in_=stg, waits=[tv], sem="st")
        else:
            tv = kb.step("vector", lambda e, tt=tt, bank=bank:
                         e.tensor_copy(kb.f32(8448 + tt * 128, 128),
                                       kb.PS[bank][:, 0:128]),
                         waits=w, inc="ve")
            st_sg[grp] = tv
            dt_toks.append(tv)
        return tv

    emit_proj(kb, "l0a_wproj", 32, 18944, evac=evac_ip,
              mm_waits=[hT_tok], load0_waits=[hT_tok], load1_waits=[],
              wb_a=WA, wb_b=WBB)
    ipA_done = [kb.tok("st"), kb.tok("ve"), kb.tok("se")]
    return emit_l0attn_rest(kb, xin, xout, h_dram, sg_dram, hbc_dram, co_dram,
                            y_dram, ipA_done, dt_toks, LN1)


def emit_l0attn_rest(kb, xin, xout, h_dram, sg_dram, hbc_dram, co_dram,
                     y_dram, ipA_done, dt_toks, LN1):
    stage = int(os.environ.get("KDBG_L0A_STAGE", "9"))
    parts = int(os.environ.get("KDBG_L0A_PARTS", "7"))
    if stage < 2:
        return ipA_done[0]

    HALO = 0
    HHT = [10240, 11280]
    CO = [12320, 13344]
    HST = 14368
    cw_in = kb.inp("l0a_convw", [128, 320])
    cb_in = kb.inp("l0a_convb", [128, 80])
    pl_in = kb.inp("l0a_place", [16, 8])
    dtb_in = kb.inp("l0a_dtb", [128, 128])
    CW = kb.f32(0, 320)
    CB = kb.f32(320, 80)
    PL = kb.AF[0:16, 400:408]
    DTB = kb.f32(408, 128)
    ld_cw = kb.dma("sync", out=CW, in_=cw_in, sem="a")
    ld_cb = kb.dma("sync", out=CB, in_=cb_in, sem="a")
    ld_pl = kb.dma("sync", out=PL, in_=pl_in, sem="a")
    ld_dtb = kb.dma("sync", out=DTB, in_=dtb_in, sem="a")

    cc_in = kb.dram("l0a_ccin", [128, CONVD], BF16)
    halo_rs = kb.dram("l0a_halors", [16, CONVD], BF16)

    spill_done = ipA_done[0]
    if not (parts & 1):
        return spill_done
    halo_ld = kb.dma("sync", out=kb.ABF[0:16, HALO:HALO + CONVD],
                     in_=hbc_dram[T - 16:T, :], waits=ipA_done, sem="a")
    tok = None
    place_dmas = []
    for b in range(8):
        stg = kb.ABF[0:16, HST:HST + CONVD]
        w = [halo_ld, ld_pl] if b == 0 else [tok]
        tv = kb.step("vector", lambda e, stg=stg, b=b:
                     e.tensor_scalar_mul(stg, kb.ABF[0:16, HALO:HALO + CONVD],
                                         PL[:, b:b + 1]),
                     waits=w + place_dmas[-1:], inc="ve")
        tok = kb.dma("gpsimd", out=cc_in[b * 16:(b + 1) * 16, :], in_=stg,
                     waits=[tv], sem="g")
        place_dmas.append(tok)
    rs_tok = kb.step("gpsimd", lambda e: e.collective_compute(
        "ReduceScatter", ALU.add, replica_groups=[list(range(NCORES))],
        ins=[cc_in.opt()], outs=[halo_rs.opt()]),
        waits=[place_dmas[-1]], inc="cc")

    if not (parts & 2):
        return rs_tok
    co_st = {}
    acc = [kb.f32(1024, 1024), kb.f32(2048, 1024)]
    conv_done = []
    for ct in range(80):
        slot = ct % 2
        w_h = [rs_tok] if ct < 2 else [co_st[ct - 2]]
        l1 = kb.dma("sync", out=kb.bf(HHT[slot], 16),
                    in_=halo_rs[:, ct * 128:(ct + 1) * 128],
                    waits=w_h, sem="a", transpose=True)
        l2 = kb.dma("sync", out=kb.bf(HHT[slot] + 16, 1024),
                    in_=hbc_dram[:, ct * 128:(ct + 1) * 128],
                    waits=[], sem="a", transpose=True)
        cw0 = ct * 4

        def tap(i, src, dst):
            def f(e, i=i, src=src, dst=dst, ct=ct, slot=slot, cw0=cw0):
                x = kb.bf(HHT[slot] + 13 + i, 1024)
                if i == 0:
                    return e.scalar_tensor_tensor(
                        dst, x, CW[:, cw0:cw0 + 1],
                        CB[:, ct:ct + 1].to_broadcast([128, 1024]),
                        ALU.mult, ALU.add)
                return e.scalar_tensor_tensor(
                    dst, x, CW[:, cw0 + i:cw0 + i + 1], src,
                    ALU.mult, ALU.add)
            return f
        tc = kb.chain("vector", [
            tap(0, None, acc[0]), tap(1, acc[0], acc[1]),
            tap(2, acc[1], acc[0]), tap(3, acc[0], acc[1]),
        ], waits=[l1, l2, ld_cw, ld_cb] + conv_done[-1:])
        ts = kb.step("scalar", lambda e, slot=slot:
                     e.activation(kb.bf(CO[slot], 1024), acc[1], AF.Silu),
                     waits=[tc], inc="se")
        co_st[ct] = kb.dma("sync", out=co_dram[ct * 128:(ct + 1) * 128, :],
                           in_=kb.bf(CO[slot], 1024), waits=[ts], sem="st")
        conv_done.append(ts)

    one_ap, one_tok = kb.ccol(1.0)
    sp_toks = []
    chunks_sp = range(NCH) if (parts & 4) else range(0)
    for c in chunks_sp:
        dtc = kb.f32(8448 + c * 128, 128)
        t0 = kb.f32(3072, 128)
        ab = kb.f32(3200, 128)
        ab2 = kb.f32(3584, 128)
        lg = kb.f32(3328, 128)
        rr = kb.f32(3456, 128)
        w0 = [ld_dtb] + dt_toks + sp_toks[-1:]
        t1 = kb.step("vector", lambda e, dtc=dtc, t0=t0:
                     e.tensor_add(t0, dtc, DTB), waits=w0, inc="ve")
        t2 = kb.step("scalar", lambda e, ab=ab, t0=t0:
                     e.activation(ab, t0, AF.Abs),
                     waits=[t1], inc="se")
        neg1_ap, neg1_tok = kb.ccol(-1.0)
        t3 = kb.step("scalar", lambda e, ab=ab, ab2=ab2, neg1_ap=neg1_ap:
                     e.activation(ab2, ab, AF.Exp, scale=neg1_ap),
                     waits=[t2, neg1_tok], inc="se")
        t4 = kb.step("scalar", lambda e, lg=lg, ab2=ab2:
                     e.activation(lg, ab2, AF.Ln, bias=one_ap),
                     waits=[t3, one_tok], inc="se")
        t5 = kb.step("vector", lambda e, rr=rr, t0=t0:
                     e.tensor_scalar_max(rr, t0, 0.0), waits=[t1], inc="ve")
        t6 = kb.step("vector", lambda e, dtc=dtc, rr=rr, lg=lg:
                     e.tensor_add(dtc, rr, lg), waits=[t4, t5], inc="ve")
        sp_toks.append(t6)

    conv_all = co_st[79]
    sp_all = sp_toks[-1]
    if stage < 3:
        return conv_all

    TOKT = [0, 10240]
    BCF = [20480, 22528]
    STATEB = 24576
    PREV = 40960
    RHSA = 49152
    HDP = 57344
    SGL = 57856
    MM = 59904
    LB = 60928
    LB2 = 61440
    T2Q = [61952, 62464]
    EAo = 62976
    DSC = 63104
    DT2 = 63232
    ABC8 = 63360
    SUFB = 64384
    DECR = 65408
    DECBC = 66432
    YT1 = 66560
    DPSF = 67072
    YGO = 75264
    SFJB = 0
    assert YGO + 1024 <= ABF_COLS
    STATE = kb.ABF.ap().bitcast(F32)[:, STATEB // 2: STATEB // 2 + 8192]
    SFJ = kb.ABF.ap().bitcast(F32)[:, 0:8192]

    trili_in = kb.inp("c_trili_bf", [128, 128], BF16)
    trili_f_in = kb.inp("c_trili_f32", [128, 128])
    trius_in = kb.inp("c_trius_bf", [128, 128], BF16)
    ones_in = kb.inp("c_ones_bf", [128, 128], BF16)
    nones_in = kb.inp("c_negones_bf", [128, 128], BF16)
    id_in = kb.inp("c_id_bf", [128, 128], BF16)
    alog_in = kb.inp("l0a_alog", [128, 128])
    dp_in = kb.inp("l0a_dp", [128, DINNER])
    cm_in = kb.inp("l0a_cm", [1, 16])
    TRI = kb.bf(76288, 128)
    TRU = kb.bf(76416, 128)
    ONE = kb.bf(76544, 128)
    NON = kb.bf(76672, 128)
    IDT = kb.bf(76800, 128)
    TRIF = kb.f32(9472, 128)
    ld_c1 = kb.dma("gpsimd", out=TRI, in_=trili_in, sem="g")
    ld_c2 = kb.dma("gpsimd", out=TRU, in_=trius_in, sem="g")
    ld_c3 = kb.dma("gpsimd", out=ONE, in_=ones_in, sem="g")
    ld_c4 = kb.dma("gpsimd", out=NON, in_=nones_in, sem="g")
    ld_c5 = kb.dma("gpsimd", out=IDT, in_=id_in, sem="g")
    ld_c6 = kb.dma("sync", out=TRIF, in_=trili_f_in, sem="a")
    ld_dps = kb.dma("gpsimd", out=kb.bf(DPSF, DINNER), in_=dp_in, sem="g")
    CM = kb.AF[0:1, 9600:9616]
    ld_cm = kb.dma("sync", out=CM, in_=cm_in, sem="a")
    AT = kb.f32(9616, 128)
    ld_al = kb.dma("sync", out=AT, in_=alog_in, sem="a")
    t_a1 = kb.step("scalar", lambda e: e.activation(AT, AT, AF.Exp),
                   waits=[ld_al], inc="se")
    t_an = kb.step("vector", lambda e: e.tensor_scalar_mul(AT, AT, -1.0),
                   waits=[t_a1], inc="ve")

    ps0 = kb.PS[0]
    ab_toks = []
    sum_mm = []
    for c in range(NCH):
        tb = kb.step("vector", lambda e, c=c: e.tensor_tensor(
            kb.bf(ABC8 + c * 128, 128), kb.f32(8448 + c * 128, 128),
            AT, ALU.mult), waits=[t_an, sp_toks[c]], inc="ve")
        ab_toks.append(tb)
        m = kb.step("tensor", lambda e, c=c: e.matmul(
            ps0[0:1, (c % 4) * 128:(c % 4) * 128 + 128], ONE[:, 0:1],
            kb.bf(ABC8 + c * 128, 128), start=(c % 4 == 0), stop=(c % 4 == 3)),
            waits=[tb, ld_c3] + ([kb.tok("ve")] if c == 4 else []), inc="pe")
        sum_mm.append(m)
        if c == 3:
            sr_ev1 = kb.step("vector", lambda e: e.tensor_copy(
                kb.AF[0:1, 4096:4608], ps0[0:1, 0:512]),
                waits=[m], inc="ve")
    SR = kb.AF[0:1, 4096:5120]
    SUF = kb.AF[0:1, 5120:6144]
    TOT = kb.AF[0:1, 6144:6272]
    t_sr = kb.step("vector", lambda e: e.tensor_copy(
        kb.AF[0:1, 4608:5120], ps0[0:1, 0:512]),
        waits=[sum_mm[-1], sr_ev1], inc="ve")
    sufs = [kb.step("vector", lambda e: e.memset(kb.AF[0:1, 5120 + 7 * 128:6144], 0.0),
                    waits=[t_sr], inc="ve")]
    for c in range(6, -1, -1):
        sufs.append(kb.step("vector", lambda e, c=c: e.tensor_add(
            kb.AF[0:1, 5120 + c * 128:5120 + (c + 1) * 128],
            kb.AF[0:1, 5120 + (c + 1) * 128:5120 + (c + 2) * 128],
            kb.AF[0:1, 4096 + (c + 1) * 128:4096 + (c + 2) * 128]),
            waits=[sufs[-1]], inc="ve"))
    t_sufb = kb.step("vector", lambda e: e.tensor_copy(
        kb.ABF[0:1, SUFB:SUFB + 1024], SUF), waits=[sufs[-1]], inc="ve")
    t_decr = kb.step("scalar", lambda e: e.activation(
        kb.ABF[0:1, DECR:DECR + 1024], SR, AF.Exp), waits=[t_sr], inc="se")
    t_tot = kb.chain("vector", [
        lambda e: e.tensor_add(TOT, kb.AF[0:1, 5120:5248], kb.AF[0:1, 4096:4224]),
    ], waits=[sufs[-1]])
    t_dtot = kb.step("scalar", lambda e: e.activation(TOT, TOT, AF.Exp),
                     waits=[t_tot], inc="se")

    c1_ld = {}
    c1_done = []
    st_wave_toks = []
    for c in range(NCH):
        slot = c % 2
        w = [conv_all] if c < 2 else [c1_done[c - 2]]
        l1 = kb.dma("sync", out=kb.bf(TOKT[slot], 9216),
                    in_=co_dram[0:9216, c * 128:(c + 1) * 128],
                    waits=w, sem="a", transpose=True)
        psd = kb.PS[1]
        md = kb.step("tensor", lambda e, c=c: [
            e.matmul(psd[:, 0:128], TRU, kb.bf(ABC8 + c * 128, 128),
                     start=True, stop=False),
            e.matmul(psd[:, 0:128], ONE[0:1, :],
                     kb.ABF[0:1, SUFB + c * 128:SUFB + (c + 1) * 128],
                     start=False, stop=True)],
            waits=[ab_toks[c], ld_c2, t_sufb] + c1_done[-1:], inc="pe")
        t_dsg = kb.step("scalar", lambda e: e.activation(
            kb.bf(DSC, 128), psd[:, 0:128], AF.Exp), waits=[md], inc="se")
        t_dt2 = kb.step("vector", lambda e, c=c: e.tensor_tensor(
            kb.bf(DT2, 128), kb.f32(8448 + c * 128, 128), kb.bf(DSC, 128),
            ALU.mult), waits=[t_dsg, sp_toks[c]], inc="ve")
        t_hc = kb.step("vector", lambda e, slot=slot: e.tensor_tensor(
            kb.bf(RHSA, DINNER).rearrange("p (h q) -> p h q", q=PDIM),
            kb.bf(TOKT[slot], DINNER).rearrange("p (h q) -> p h q", q=PDIM),
            kb.bf(DT2, 128).unsqueeze(2).to_broadcast([128, 128, PDIM]),
            ALU.mult), waits=[t_dt2, l1], inc="ve")
        for wv in range(8):
            bks = (2, 3) if wv % 2 == 0 else (4, 5)
            mm_w = [t_hc]
            if len(st_wave_toks) >= 2:
                mm_w.append(st_wave_toks[-2])

            def smm(e, c=c, slot=slot, wv=wv, bks=bks):
                r = []
                for hh in range(16):
                    h = wv * 16 + hh
                    g = h // 16
                    bank = bks[hh // 8]
                    r.append(e.matmul(
                        kb.PS[bank][:, (hh % 8) * 64:(hh % 8) * 64 + 64],
                        kb.bf(TOKT[slot] + DINNER + g * 128, 128),
                        kb.bf(RHSA + h * PDIM, PDIM),
                        start=(hh % 8 == 0), stop=(hh % 8 == 7)))
                return r
            tm = kb.step("tensor", smm, waits=mm_w, inc="pe")

            def sadd(e, c=c, wv=wv, bks=bks):
                r = []
                for half in range(2):
                    src = kb.PS[bks[half]][:, 0:512]
                    dst = STATE[:, wv * 1024 + half * 512: wv * 1024 + half * 512 + 512]
                    if c == 0:
                        r.append(e.tensor_copy(dst, src))
                    else:
                        r.append(e.tensor_add(dst, dst, src))
                return r
            ta = kb.step("vector", sadd, waits=[tm], inc="ve")
            st_wave_toks.append(ta)
        c1_done.append(st_wave_toks[-1])

    pay = kb.dram("l0a_pay", [129, DINNER], F32)
    gath = kb.dram("l0a_gath", [8 * 129, DINNER], F32)
    d_pay1 = kb.dma("sync", out=pay[0:128, :], in_=STATE,
                    waits=[c1_done[-1]], sem="st")
    d_pay2 = kb.dma("sync", out=pay[128:129, 0:128], in_=TOT, waits=[t_dtot],
                    sem="st")
    ag_tok = kb.step("gpsimd", lambda e: e.collective_compute(
        "AllGather", ALU.bypass, replica_groups=[list(range(NCORES))],
        ins=[pay.opt()], outs=[gath.opt()]),
        waits=[d_pay1, d_pay2], inc="cc")
    CMa = lambda j: kb.AF[0:1, 9600 + j:9601 + j]
    CMb = lambda j: kb.AF[0:1, 9608 + j:9609 + j]
    MDTr = kb.AF[0:1, 6272:6400]
    PRUN = kb.AF[0:1, 6400:6528]
    WJF = kb.AF[0:1, 6528:6656]
    WJB = kb.ABF[0:1, SUFB:SUFB + 128]
    t_zero = kb.step("vector", lambda e: e.memset(STATE, 0.0),
                     waits=[d_pay1], inc="ve")
    t_p = kb.step("vector", lambda e: e.memset(PRUN, 1.0), waits=[ag_tok],
                  inc="ve")
    acc_tok = t_zero
    ps_w = kb.PS[6]
    WBC = kb.bf(DSC, 128)
    for j in range(7, -1, -1):
        ldj = kb.dma("sync", out=MDTr,
                     in_=gath[j * 129 + 128:j * 129 + 129, 0:128],
                     waits=[ag_tok], sem="a")
        t_m = kb.step("vector", lambda e, j=j: e.tensor_scalar(
            MDTr, MDTr, CMa(j), CMb(j), ALU.mult, ALU.add),
            waits=[ldj, ld_cm, t_p], inc="ve")
        t_wj = kb.step("vector", lambda e, j=j: e.tensor_scalar_mul(
            WJF, PRUN, CMa(j)), waits=[t_m], inc="ve")
        t_wjb = kb.step("vector", lambda e: e.tensor_copy(WJB, WJF),
                        waits=[t_wj], inc="ve")
        t_p = kb.step("vector", lambda e: e.tensor_tensor(
            PRUN, PRUN, MDTr, ALU.mult), waits=[t_wjb], inc="ve")
        m_b = kb.step("tensor", lambda e: e.matmul(
            ps_w[:, 0:128], ONE[0:1, :], WJB, start=True, stop=True),
            waits=[t_wjb] + ([] if j == 7 else [acc_tok]), inc="pe")
        t_bc = kb.step("vector", lambda e: e.tensor_copy(WBC, ps_w[:, 0:128]),
                       waits=[m_b], inc="ve")
        ldS = kb.dma("sync", out=SFJ, in_=gath[j * 129:j * 129 + 128, :],
                     waits=[ag_tok] + ([] if j == 7 else [acc_tok]), sem="a")
        t_mu = kb.step("vector", lambda e: e.tensor_tensor(
            SFJ.rearrange("p (h q) -> p h q", q=PDIM),
            SFJ.rearrange("p (h q) -> p h q", q=PDIM),
            WBC.unsqueeze(2).to_broadcast([128, 128, PDIM]), ALU.mult),
            waits=[ldS, t_bc], inc="ve")
        acc_tok = kb.step("vector", lambda e: e.tensor_add(STATE, STATE, SFJ),
                          waits=[t_mu, t_zero], inc="ve")
    sin_tok = acc_tok

    if stage < 4:
        return sin_tok

    eps_ap, eps_tok = kb.ccol(EPS)
    GJF = 4096
    c2_last = []
    t2w = {}
    my_toks = {}
    tp2_toks = {}
    ts1_toks = {}
    tg_toks = {}
    sgl_toks = {}
    ysc_toks = {}
    yst_toks = {}
    hdp_toks = {}
    clamp_toks = {}
    tt1_toks = {}
    tex_toks = {}
    for c in range(NCH):
        slot = c % 2
        w_ld = [sin_tok] if c < 2 else [c2_last[c - 2]]
        l_tok = kb.dma("sync", out=kb.bf(TOKT[slot], CONVD),
                       in_=co_dram[:, c * 128:(c + 1) * 128],
                       waits=w_ld, sem="a", transpose=True)
        l_bcf = kb.dma("sync",
                       out=kb.bf(BCF[slot], 2048).rearrange(
                           "p (g t) -> p g t", g=16),
                       in_=co_dram[DINNER:CONVD, c * 128:(c + 1) * 128]
                       .rearrange("(g p) t -> p g t", p=128),
                       waits=w_ld, sem="a")
        t_prev = kb.step("vector", lambda e: e.tensor_copy(
            kb.bf(PREV, DINNER), STATE),
            waits=([sin_tok] if c == 0 else [c2_last[c - 1]]), inc="ve")
        psa = kb.PS[1]
        m_acs = kb.step("tensor", lambda e, c=c: e.matmul(
            psa[:, 0:128], TRI, kb.bf(ABC8 + c * 128, 128),
            start=True, stop=True),
            waits=[ld_c1, ab_toks[c]] + c2_last[-1:] +
                  ([sin_tok] if c == 0 else []), inc="pe")
        t_ea = kb.step("scalar", lambda e: e.activation(
            kb.bf(EAo, 128), psa[:, 0:128], AF.Exp), waits=[m_acs], inc="se")
        m_dsc = kb.step("tensor", lambda e, c=c: e.matmul(
            psa[:, 128:256], TRU, kb.bf(ABC8 + c * 128, 128),
            start=True, stop=True),
            waits=[ld_c2, ab_toks[c], t_ea] + ([sin_tok] if c == 0 else []),
            inc="pe")
        t_dsc = kb.step("scalar", lambda e: e.activation(
            kb.bf(DSC, 128), psa[:, 128:256], AF.Exp), waits=[m_dsc], inc="se")
        t_dt2 = kb.step("vector", lambda e, c=c: e.tensor_tensor(
            kb.bf(DT2, 128), kb.f32(8448 + c * 128, 128), kb.bf(DSC, 128),
            ALU.mult), waits=[t_dsc], inc="ve")
        psm = kb.PS[2]
        mm_evac = {}
        for wv2 in range(2):
            wmm = [l_bcf]
            if wv2 == 1:
                wmm += [mm_evac[g] for g in range(4)]

            def mmw(e, wv2=wv2, slot=slot):
                r = []
                for gi in range(4):
                    g = wv2 * 4 + gi
                    r.append(e.matmul(
                        psm[:, gi * 128:gi * 128 + 128],
                        kb.bf(BCF[slot] + g * 128, 128),
                        kb.bf(BCF[slot] + (8 + g) * 128, 128),
                        start=(gi == 0), stop=(gi == 3)))
                return r
            m_g = kb.step("tensor", mmw, waits=wmm, inc="pe")
            for gi in range(4):
                g = wv2 * 4 + gi
                mm_evac[g] = kb.step("vector", lambda e, g=g, gi=gi:
                    e.tensor_tensor(
                        kb.bf(MM + g * 128, 128),
                        psm[:, gi * 128:gi * 128 + 128],
                        TRIF, ALU.mult), waits=[m_g, ld_c6], inc="ve")
        for half in range(2):
            wh = [ld_c1, ab_toks[c]]
            if half == 1:
                wh.append(t2w[(c, 15)])
            elif c > 0:
                wh.append(c2_last[c - 1])
            rh_tok = kb.step("vector", lambda e, c=c, half=half:
                e.tensor_tensor(
                    kb.bf(RHSA, DINNER).rearrange("p (h i) -> p h i", i=128),
                    TRI.unsqueeze(1).to_broadcast([128, 64, 128]),
                    kb.bf(ABC8 + c * 128 + half * 64, 64).unsqueeze(2)
                    .to_broadcast([128, 64, 128]), ALU.mult), waits=wh, inc="ve")
            for qh in range(16):
                q = half * 16 + qh
                psd = kb.PS[3 + q % 2]
                wq = [rh_tok, ld_c3, ld_c4]
                if (c, q - 2) in clamp_toks:
                    wq.append(clamp_toks[(c, q - 2)])
                elif c > 0 and q < 2:
                    wq.append(clamp_toks[(c - 1, q + 30)])

                def dmm(e, qh=qh, psd=psd):
                    r = [e.matmul(psd[:, 0:512], ONE,
                                  kb.bf(RHSA + qh * 512, 512),
                                  start=True, stop=False)]
                    for hi in range(4):
                        r.append(e.matmul(
                            psd[:, hi * 128:hi * 128 + 128],
                            kb.bf(RHSA + qh * 512 + hi * 128, 128), NON,
                            start=False, stop=(hi == 3)))
                    return r
                m_d = kb.step("tensor", dmm, waits=wq, inc="pe")
                wcl = [m_d]
                if (c, q - 1) in t2w:
                    wcl.append(t2w[(c, q - 1)])
                elif c > 0 and q == 0:
                    wcl.append(t2w[(c - 1, 31)])
                clamp_toks[(c, q)] = kb.step(
                    "vector", lambda e, psd=psd: e.tensor_scalar_min(
                        kb.bf(LB, 512), psd[:, 0:512], 0.0),
                    waits=wcl, inc="ve")
                wex = [clamp_toks[(c, q)]]
                if (c, q - 1) in tt1_toks:
                    wex.append(tt1_toks[(c, q - 1)])
                elif c > 0 and q == 0:
                    wex.append(tt1_toks[(c - 1, 31)])
                tex_toks[(c, q)] = kb.step(
                    "scalar", lambda e: e.activation(
                        kb.bf(LB2, 512), kb.bf(LB, 512), AF.Exp),
                    waits=wex, inc="se")
                g4 = q // 4
                tt1_toks[(c, q)] = kb.step(
                    "vector", lambda e, g4=g4: e.tensor_tensor(
                        kb.bf(LB, 512).rearrange("p (f i) -> p f i", f=4),
                        kb.bf(LB2, 512).rearrange("p (f i) -> p f i", f=4),
                        kb.bf(MM + g4 * 128, 128).unsqueeze(1)
                        .to_broadcast([128, 4, 128]), ALU.mult),
                    waits=[tex_toks[(c, q)], mm_evac[g4]], inc="ve")
                wt2 = [tt1_toks[(c, q)]]
                if (c, (q - 2) // 2) in my_toks and q >= 2:
                    wt2.append(my_toks[(c, (q - 2) // 2)])
                elif c > 0 and q < 2:
                    wt2.append(my_toks[(c - 1, 15)])
                t2w[(c, q)] = kb.step(
                    "gpsimd", lambda e, q=q, c=c: e.tensor_tensor(
                        kb.bf(T2Q[q % 2], 512).rearrange("p (f i) -> p f i", f=4),
                        kb.bf(LB, 512).rearrange("p (f i) -> p f i", f=4),
                        kb.f32(8448 + c * 128 + q * 4, 4).unsqueeze(2)
                        .to_broadcast([128, 4, 128]), ALU.mult),
                    waits=wt2, inc="po")
                if q % 2 == 0:
                    continue
                grp = q // 2
                ng = grp // 2
                ngsel = ng % 2
                piece = grp % 2
                ygb = kb.f32(ngsel * 1024 + piece * 512, 512)
                ygb2 = kb.f32(2048 + ngsel * 1024 + piece * 512, 512)
                h0 = grp * 8
                bA, bB = kb.PS[5], kb.PS[6]
                whdp = [l_tok, ld_dps]
                if (c, grp - 1) in my_toks:
                    whdp.append(my_toks[(c, grp - 1)])
                elif c > 0 and grp == 0:
                    whdp.append(my_toks[(c - 1, 15)])
                hdp_toks[(c, grp)] = kb.step(
                    "gpsimd", lambda e, h0=h0, slot=slot: e.tensor_tensor(
                        kb.bf(HDP, 512), kb.bf(TOKT[slot] + h0 * PDIM, 512),
                        kb.bf(DPSF + h0 * PDIM, 512), ALU.mult),
                    waits=whdp, inc="po")

                def ymm(e, h0=h0, slot=slot, q=q, bA=bA, bB=bB):
                    r = [e.matmul(bA[:, 0:512], IDT, kb.bf(HDP, 512),
                                  start=True, stop=False)]
                    for hh in range(8):
                        tq = T2Q[(q - 1) % 2] if hh < 4 else T2Q[q % 2]
                        r.append(e.matmul(
                            bA[:, hh * 64:hh * 64 + 64],
                            kb.bf(tq + (hh % 4) * 128, 128),
                            kb.bf(TOKT[slot] + (h0 + hh) * PDIM, PDIM),
                            start=False, stop=(hh == 7)))
                    for hh in range(8):
                        g = (h0 + hh) // 16
                        r.append(e.matmul(
                            bB[:, hh * 64:hh * 64 + 64],
                            kb.bf(BCF[slot] + (8 + g) * 128, 128),
                            kb.bf(PREV + (h0 + hh) * PDIM, PDIM),
                            start=(hh == 0), stop=(hh == 7)))
                    return r
                wy = [t2w[(c, q)], t2w[(c, q - 1)], hdp_toks[(c, grp)],
                      t_prev, ld_c5]
                if (c, grp - 1) in tp2_toks:
                    wy += [tp2_toks[(c, grp - 1)], ts1_toks[(c, grp - 1)]]
                elif c > 0 and grp == 0:
                    wy += [tp2_toks[(c - 1, 15)], ts1_toks[(c - 1, 15)]]
                my_toks[(c, grp)] = kb.step("tensor", ymm, waits=wy, inc="pe")
                wsg = []
                if (c, grp - 4) in tg_toks:
                    wsg.append(tg_toks[(c, grp - 4)])
                elif c > 0 and grp < 4:
                    wsg.append(tg_toks[(c - 1, grp + 12)])
                sgl_toks[(c, grp)] = kb.dma(
                    "sync", out=kb.bf(SGL + (grp % 4) * 512, 512),
                    in_=sg_dram[c * 128:(c + 1) * 128, h0 * PDIM:h0 * PDIM + 512],
                    waits=wsg, sem="a")
                ts1_toks[(c, grp)] = kb.step(
                    "vector", lambda e, h0=h0, bB=bB: e.tensor_tensor(
                        kb.bf(YT1, 512).rearrange("p (h w) -> p h w", w=PDIM),
                        bB[:, 0:512].rearrange("p (h w) -> p h w", w=PDIM),
                        kb.bf(EAo, 128)[:, h0:h0 + 8].unsqueeze(2)
                        .to_broadcast([128, 8, PDIM]), ALU.mult),
                    waits=[my_toks[(c, grp)], t_ea], inc="ve")
                wp2 = [ts1_toks[(c, grp)]]
                if (c, (ng - 2) * 2 + piece) in tg_toks:
                    wp2.append(tg_toks[(c, (ng - 2) * 2 + piece)])
                elif c > 0 and ng < 2:
                    wp2.append(tg_toks[(c - 1, (ng + 6) * 2 + piece)])
                tp2_toks[(c, grp)] = kb.step(
                    "vector", lambda e, ygb=ygb, bA=bA: e.tensor_add(
                        ygb, kb.bf(YT1, 512), bA[:, 0:512]),
                    waits=wp2, inc="ve")
                wg = [tp2_toks[(c, grp)], sgl_toks[(c, grp)]]
                if (c, ng - 2) in ysc_toks:
                    wg.append(ysc_toks[(c, ng - 2)])
                elif c > 0 and ng < 2:
                    wg.append(ysc_toks[(c - 1, ng + 6)])
                tg_toks[(c, grp)] = kb.step(
                    "gpsimd", lambda e, ygb=ygb, ygb2=ygb2, grp=grp:
                    e.tensor_tensor(ygb2, ygb,
                                    kb.bf(SGL + (grp % 4) * 512, 512),
                                    ALU.mult), waits=wg, inc="po")
                if piece == 1:
                    ssq = kb.f32(8192 + ng % 2, 1)
                    rsd = kb.f32(8200 + ng % 2, 1)
                    yg2full = kb.f32(2048 + ngsel * 1024, 1024)
                    t_sq = kb.step("vector", lambda e, yg2full=yg2full, ssq=ssq:
                        e.scalar_tensor_tensor(
                            kb.f32(GJF, 1024), yg2full, 1.0, yg2full,
                            ALU.mult, ALU.mult, accum_out=ssq),
                        waits=[tg_toks[(c, grp)], tg_toks[(c, grp - 1)]],
                        inc="ve")
                    t_r1 = kb.step("vector", lambda e, ssq=ssq, rsd=rsd:
                        e.tensor_scalar_mul(rsd, ssq, 1.0 / 1024.0),
                        waits=[t_sq], inc="ve")
                    t_r2 = kb.step("scalar", lambda e, rsd=rsd: e.activation(
                        rsd, rsd, AF.Sqrt, bias=eps_ap),
                        waits=[t_r1, eps_tok], inc="se")
                    t_r3 = kb.step("vector", lambda e, rsd=rsd:
                        e.reciprocal(rsd, rsd), waits=[t_r2], inc="ve")
                    wsc = [t_r3]
                    if (c, ng - 1) in yst_toks:
                        wsc.append(yst_toks[(c, ng - 1)])
                    elif c > 0 and ng == 0:
                        wsc.append(yst_toks[(c - 1, 7)])
                    ysc_toks[(c, ng)] = kb.step(
                        "vector", lambda e, yg2full=yg2full, rsd=rsd:
                        e.tensor_scalar_mul(kb.bf(YGO, 1024), yg2full, rsd),
                        waits=wsc, inc="ve")
                    yst_toks[(c, ng)] = kb.dma(
                        "sync", out=y_dram[c * 128:(c + 1) * 128,
                                           ng * 1024:(ng + 1) * 1024],
                        in_=kb.bf(YGO, 1024), waits=[ysc_toks[(c, ng)]],
                        sem="st")
        t_hc2 = kb.step("vector", lambda e, c=c, slot=slot: e.tensor_tensor(
            kb.bf(RHSA, DINNER).rearrange("p (h w) -> p h w", w=PDIM),
            kb.bf(TOKT[slot], DINNER).rearrange("p (h w) -> p h w", w=PDIM),
            kb.bf(DT2, 128).unsqueeze(2).to_broadcast([128, 128, PDIM]),
            ALU.mult), waits=[t2w[(c, 31)], t_dt2, l_tok], inc="ve")
        ps7 = kb.PS[7]
        m_dec = kb.step("tensor", lambda e, c=c: e.matmul(
            ps7[:, 0:128], ONE[0:1, :],
            kb.ABF[0:1, DECR + c * 128:DECR + (c + 1) * 128],
            start=True, stop=True), waits=[t_decr, ld_c3] + c2_last[-1:],
            inc="pe")
        t_dbc = kb.step("vector", lambda e: e.tensor_copy(
            kb.bf(DECBC, 128), ps7[:, 0:128]), waits=[m_dec], inc="ve")
        t_smul = kb.step("vector", lambda e: e.tensor_tensor(
            STATE.rearrange("p (h w) -> p h w", w=PDIM),
            STATE.rearrange("p (h w) -> p h w", w=PDIM),
            kb.bf(DECBC, 128).unsqueeze(2).to_broadcast([128, 128, PDIM]),
            ALU.mult), waits=[t_dbc, t_prev], inc="ve")
        wv_tok = t_smul
        for wv in range(8):
            bks = (3, 4)

            def smm(e, slot=slot, wv=wv, bks=bks):
                r = []
                for hh in range(16):
                    h = wv * 16 + hh
                    g = h // 16
                    bank = bks[hh // 8]
                    r.append(e.matmul(
                        kb.PS[bank][:, (hh % 8) * 64:(hh % 8) * 64 + 64],
                        kb.bf(TOKT[slot] + DINNER + g * 128, 128),
                        kb.bf(RHSA + h * PDIM, PDIM),
                        start=(hh % 8 == 0), stop=(hh % 8 == 7)))
                return r
            wsm = [t_hc2]
            if wv == 0:
                wsm.append(clamp_toks[(c, 31)])
            else:
                wsm.append(wv_tok)
            tm = kb.step("tensor", smm, waits=wsm, inc="pe")

            def sadd(e, wv=wv, bks=bks):
                r = []
                for hf in range(2):
                    r.append(e.tensor_add(
                        STATE[:, wv * 1024 + hf * 512:wv * 1024 + hf * 512 + 512],
                        STATE[:, wv * 1024 + hf * 512:wv * 1024 + hf * 512 + 512],
                        kb.PS[bks[hf]][:, 0:512]))
                return r
            wv_tok = kb.step("vector", sadd, waits=[tm, t_smul], inc="ve")
        c2_last.append(wv_tok)

    if stage < 5:
        return yst_toks[(7, 7)]

    barrier = kb.barrier_tokens()
    g0c_in = kb.inp("l0a_g0col", [128, 64])
    G0C = kb.f32(4096, 64)
    ld_g0 = kb.dma("sync", out=G0C, in_=g0c_in, waits=barrier, sem="a")
    STG2 = [65536, 66560]
    STG2 = [75264, 76288]
    yT_toks = {}
    sc_toks = {}
    for kt in range(64):
        wtt = [barrier[0]] if kt == 0 else []
        if kt >= 2:
            wtt.append(sc_toks[kt - 2])
        yT_toks[kt] = kb.dma("sync", out=kb.bf(STG2[kt % 2], T),
                             in_=y_dram[:, kt * 128:(kt + 1) * 128],
                             waits=wtt + [yst_toks[(7, 7)]], sem="a",
                             transpose=True)
        sc_toks[kt] = kb.step("vector", lambda e, kt=kt: e.tensor_scalar_mul(
            kb.bf(kt * T, T), kb.bf(STG2[kt % 2], T), G0C[:, kt:kt + 1]),
            waits=[yT_toks[kt], ld_g0] + ([] if kt else barrier), inc="ve")
    yT_done = sc_toks[63]

    ep_x = [kb.f32(0 + i * 128, 128) for i in range(4)]
    ep_z = [kb.f32(512 + i * 128, 128) for i in range(4)]
    ep_t = [kb.f32(1024 + i * 128, 128) for i in range(4)]
    st_toks2 = {}
    ld_toks2 = {}

    def evac_op(b, tt, grp, bank, pe_tok):
        slotp = grp % 4
        w = []
        if grp >= 4:
            w.append(st_toks2[grp - 4])
        ld_toks2[grp] = kb.dma(
            "sync", out=ep_x[slotp],
            in_=xin[tt * 128:(tt + 1) * 128, b * 128:(b + 1) * 128],
            waits=w, sem="a")
        rs1 = kb.f32(LN1 + 2 * tt, 1)
        nb1 = kb.f32(LN1 + 2 * tt + 1, 1)
        t_z = kb.step("scalar", lambda e, slotp=slotp, rs1=rs1, nb1=nb1:
                      e.activation(ep_z[slotp], ep_x[slotp], AF.Identity,
                                   bias=nb1, scale=rs1),
                      waits=[ld_toks2[grp]], inc="se")
        t_a = kb.step("vector", lambda e, slotp=slotp, bank=bank:
                      e.tensor_add(ep_t[slotp], ep_x[slotp],
                                   kb.PS[bank][:, 0:128]),
                      waits=[pe_tok, ld_toks2[grp], t_z], inc="ve")
        t_o = kb.step("vector", lambda e, slotp=slotp:
                      e.tensor_add(ep_t[slotp], ep_t[slotp], ep_z[slotp]),
                      waits=[t_a], inc="ve")
        st_toks2[grp] = kb.dma(
            "sync", out=xout[tt * 128:(tt + 1) * 128, b * 128:(b + 1) * 128],
            in_=ep_t[slotp], waits=[t_o], sem="st")
        return t_a

    emit_proj(kb, "l0a_woT", 64, D, evac=evac_op, bw=128,
              lhsT=lambda kt, tt: kb.bf(kt * T + tt * 128, 128),
              mm_waits=[yT_done],
              load0_waits=barrier, load1_waits=[yT_done],
              wb_a=65536, wb_b=65536 + 8192)
    return kb.tok("st")


def emit_l1ffn(kb, xin_name, xout_name, x_ready, first=False):
    xin = kb.inp(xin_name, [T, D]) if first else kb.dram(xin_name, [T, D], F32)
    xout = (kb.out(xout_name, [T, D]) if xout_name == "xout"
            else kb.dram(xout_name, [T, D], F32))
    h_dram = kb.dram("h_dram", [T, D], BF16)
    zg_dram = kb.dram("zg_dram", [T, 512], BF16)

    hT_tok = emit_ln(kb, xin, scale_mul=1.0, waits=x_ready, h_dram=h_dram)

    WQT = SC_OFF
    KEYS = SC_OFF + 512
    QRT = SC_OFF + 576
    ZGT = SC_OFF + 1600
    ZGS = SC_OFF + 5696
    assert ZGS + 1024 <= ABF_COLS

    wq_in = kb.inp("l1f_WqT", [D, 16])
    keys_in = kb.inp("l1f_keysT", [16, 64])
    ld_wq = kb.dma("gpsimd", out=kb.bf(WQT, 512).rearrange(
        "p (kt o) -> p kt o", kt=32),
        in_=wq_in.rearrange("(kt p) o -> p kt o", p=128), sem="g")
    ld_keys = kb.dma("gpsimd", out=kb.ABF[0:16, KEYS:KEYS + 64],
                     in_=keys_in, sem="g")

    qr_toks = []
    for hh in range(2):
        ps = kb.PS[4]

        def mmq(e, hh=hh, ps=ps):
            r = []
            for kt in range(32):
                r.append(e.matmul(ps[0:16, :], kb.bf(WQT + kt * 16, 16),
                                  kb.bf(HT_OFF + kt * T + hh * 512, 512),
                                  start=(kt == 0), stop=(kt == 31)))
            return r
        tm = kb.step("tensor", mmq, waits=[ld_wq, hT_tok] + qr_toks, inc="pe")
        te = kb.step("vector", lambda e, hh=hh, ps=ps:
                     e.tensor_copy(kb.ABF[0:16, QRT + hh * 512:QRT + hh * 512 + 512],
                                   ps[0:16, :]),
                     waits=[tm], inc="ve")
        qr_toks.append(te)

    gate_toks = {}
    sc_evac_toks = []
    for tt in range(TT):
        ps = kb.PS[5]
        w = [ld_keys] + qr_toks + sc_evac_toks[-1:]

        def mms(e, tt=tt, ps=ps):
            return e.matmul(ps[:, 0:64], kb.ABF[0:16, QRT + tt * 128:QRT + tt * 128 + 128],
                            kb.ABF[0:16, KEYS:KEYS + 64], start=True, stop=True)
        tm = kb.step("tensor", mms, waits=w, inc="pe")
        sc = kb.f32(8448 + tt * 64, 64)
        gcol = kb.f32(8960 + tt * 64, 64)
        tiny = 9472 + tt * 8
        negm = kb.f32(tiny, 1)
        thr = kb.f32(tiny + 1, 1)
        zsum = kb.f32(tiny + 2, 1)
        rz = kb.f32(tiny + 3, 1)
        mx = kb.f32(tiny + 4, 1)
        t8 = kb.f32(10000 + (tt % 2) * 24, 8)
        g2col = kb.f32(9536 + (tt % 2) * 64, 64)
        ev = kb.step("vector", lambda e, sc=sc, ps=ps:
                     e.tensor_copy(sc, ps[:, 0:64]), waits=[tm], inc="ve")
        sc_evac_toks.append(ev)

        def c1(e, t8=t8, sc=sc):
            return e.max(out=t8, in_=sc)
        def c2(e, t8=t8, sc=sc, g2col=g2col):
            return e.match_replace(out=g2col, in_to_replace=t8, in_values=sc,
                                   imm_value=-1e30)
        def c3(e, t8=t8, g2col=g2col):
            return e.max(out=t8, in_=g2col)
        def c4(e, thr=thr, t8=t8):
            return e.tensor_copy(thr, t8[:, 7:8])
        tk = kb.chain("vector", [c1, c2, c3, c4], waits=[ev])
        mk = kb.chain("vector", [
            lambda e, mx=mx, sc=sc: e.reduce_max(mx, sc, axis=AX.X),
            lambda e, negm=negm, mx=mx: e.tensor_scalar_mul(negm, mx, -1.0),
        ], waits=[ev])
        es = kb.step("scalar", lambda e, gcol=gcol, sc=sc, negm=negm:
                     e.activation(gcol, sc, AF.Exp, bias=negm),
                     waits=[tk, mk], inc="se")
        def g1(e, g2col=g2col, sc=sc, thr=thr, gcol=gcol):
            return e.scalar_tensor_tensor(g2col, sc, thr, gcol,
                                          ALU.is_ge, ALU.mult)
        def g2(e, zsum=zsum, g2col=g2col):
            return e.reduce_sum(zsum, g2col, axis=AX.X)
        def g3(e, rz=rz, zsum=zsum):
            return e.reciprocal(rz, zsum)
        def g4(e, gcol=gcol, g2col=g2col, rz=rz):
            return e.tensor_scalar_mul(gcol, g2col, rz)
        gate_toks[tt] = kb.chain("vector", [g1, g2, g3, g4], waits=[es])

    zg_st = {}

    def evac_z(b, tt, grp, bank, pe_tok):
        slot = grp % 2
        zgs = kb.bf(ZGS + slot * 512, 512)
        w = [pe_tok, gate_toks[tt]]
        if grp >= 2:
            w.append(zg_st[grp - 2])
        gcol = kb.f32(8960 + tt * 64, 64)
        del b
        tv = kb.step("vector", lambda e, zgs=zgs, bank=bank, gcol=gcol:
                     e.tensor_tensor(
                         zgs.rearrange("p (e r) -> p e r", r=8),
                         kb.PS[bank][:, 0:512].rearrange("p (e r) -> p e r", r=8),
                         gcol.unsqueeze(2).to_broadcast([128, 64, 8]),
                         ALU.mult),
                     waits=w, inc="ve")
        zg_st[grp] = kb.dma("sync", out=zg_dram[tt * 128:(tt + 1) * 128, :],
                            in_=zgs, waits=[tv], sem="st")
        return tv

    pe_z, _ = emit_proj(kb, "l1f_AwT", 32, 512, evac=evac_z,
                        mm_waits=[hT_tok], load0_waits=[hT_tok],
                        psum_banks=(0, 1, 2, 3))
    z_done = kb.tok("st")

    zgT_tok = None
    for k in range(4):
        zgT_tok = kb.dma("sync", out=kb.bf(ZGT + k * T, T),
                         in_=zg_dram[:, k * 128:(k + 1) * 128],
                         waits=[z_done] if k == 0 else (), sem="a",
                         transpose=True)

    ep_x = [kb.f32(0 + i * 512, 512) for i in range(4)]
    ep_o = [kb.f32(4096 + i * 512, 512) for i in range(4)]
    st_toks = {}
    ld_toks = {}

    def evac_b(b, tt, grp, bank, pe_tok):
        slot = grp % 4
        w = []
        if grp >= 4:
            w.append(st_toks[grp - 4])
        ld_toks[grp] = kb.dma(
            "sync", out=ep_x[slot],
            in_=xin[tt * 128:(tt + 1) * 128, b * 512:(b + 1) * 512],
            waits=w, sem="a")
        tv = kb.step("vector", lambda e, slot=slot, bank=bank:
                     e.tensor_add(ep_o[slot], ep_x[slot], kb.PS[bank][:, 0:512]),
                     waits=[pe_tok, ld_toks[grp]], inc="ve")
        st_toks[grp] = kb.dma(
            "sync", out=xout[tt * 128:(tt + 1) * 128, b * 512:(b + 1) * 512],
            in_=ep_o[slot], waits=[tv], sem="st")
        return tv

    emit_proj(kb, "l1f_BwT", 4, D, evac=evac_b,
              lhsT=lambda kt, tt: kb.bf(ZGT + kt * T + tt * 128, 128),
              mm_waits=[zgT_tok],
              load0_waits=[pe_z[(0, TT - 1)]],
              load1_waits=[pe_z[(0, TT - 1)]])
    return kb.tok("st")


H1 = 32
KVH = 8
DH = 128


def emit_l1attn(kb, xin_name, xout_name, x_ready, first=False):
    xin = kb.inp(xin_name, [T, D]) if first else kb.dram(xin_name, [T, D], F32)
    xout = (kb.out(xout_name, [T, D]) if xout_name == "xout"
            else kb.dram(xout_name, [T, D], F32))
    h_dram = kb.dram("h_dram", [T, D], BF16)
    q_dram = kb.dram("q_dram", [T, 4096], BF16)
    k_dram = kb.dram("k_dram", [T, 1024], BF16)
    v_dram = kb.dram("v_dram", [T, 1024], BF16)
    sg1_dram = kb.dram("sg1_dram", [T, 4096], BF16)
    y1_dram = kb.dram("y1_dram", [T, 4096], BF16)

    hT_tok = emit_ln(kb, xin, scale_mul=1.0, waits=x_ready, h_dram=h_dram)

    cos_in = kb.inp("l1_cos", [T, 64])
    sin_in = kb.inp("l1_sin", [T, 64])
    COS = kb.f32(0, 512)
    SIN = kb.f32(512, 512)
    ld_cos = kb.dma("sync", out=COS.rearrange("p (t w) -> p t w", w=64),
                    in_=cos_in.rearrange("(t p) w -> p t w", p=128),
                    waits=list(x_ready), sem="a")
    ld_sin = kb.dma("sync", out=SIN.rearrange("p (t w) -> p t w", w=64),
                    in_=sin_in.rearrange("(t p) w -> p t w", p=128),
                    waits=list(x_ready), sem="a")
    STG = 69632
    RP1 = 73728
    dt_toks = []
    st_q = {}

    def evac_qk(b, tt, grp, bank, pe_tok, dst, dstcol):
        slot = grp % 8
        stg = kb.bf(STG + slot * 512, 512)
        r1 = kb.bf(RP1, 256)
        r2 = kb.bf(RP1 + 256, 256)
        w = [pe_tok, ld_cos, ld_sin]
        if grp >= 8:
            w.append(st_q[grp - 8])
        ps = kb.PS[bank]
        csl = lambda off: kb.f32(off + tt * 64, 64).unsqueeze(1).to_broadcast(
            [128, 4, 64])
        pL = ps[:, 0:512].rearrange("p (f w) -> p f w", w=128)[:, :, 0:64]
        pR = ps[:, 0:512].rearrange("p (f w) -> p f w", w=128)[:, :, 64:128]
        sL = stg.rearrange("p (f w) -> p f w", w=128)[:, :, 0:64]
        sR = stg.rearrange("p (f w) -> p f w", w=128)[:, :, 64:128]
        r1v = r1.rearrange("p (f w) -> p f w", w=64)
        r2v = r2.rearrange("p (f w) -> p f w", w=64)
        t1 = kb.step("vector", lambda e: e.tensor_tensor(
            r1v, pL, csl(0), ALU.mult), waits=w, inc="ve")
        t2 = kb.step("vector", lambda e: e.tensor_tensor(
            r2v, pR, csl(512), ALU.mult), waits=[t1], inc="ve")
        t3 = kb.step("vector", lambda e: e.tensor_tensor(
            sL, r1v, r2v, ALU.subtract), waits=[t2], inc="ve")
        t4 = kb.step("vector", lambda e: e.tensor_tensor(
            r1v, pR, csl(0), ALU.mult), waits=[t3], inc="ve")
        t5 = kb.step("vector", lambda e: e.tensor_tensor(
            r2v, pL, csl(512), ALU.mult), waits=[t4], inc="ve")
        t6 = kb.step("vector", lambda e: e.tensor_tensor(
            sR, r1v, r2v, ALU.add), waits=[t5], inc="ve")
        st_q[grp] = kb.dma(
            "sync", out=dst[tt * 128:(tt + 1) * 128, dstcol:dstcol + 512],
            in_=stg, waits=[t6], sem="st")
        return t6

    def evac_l1(b, tt, grp, bank, pe_tok):
        slot = grp % 8
        stg = kb.bf(STG + slot * 512, 512)
        w = [pe_tok]
        if grp >= 8:
            w.append(st_q[grp - 8])
        if b < 8:
            return evac_qk(b, tt, grp, bank, pe_tok, q_dram, b * 512)
        if b < 10:
            return evac_qk(b, tt, grp, bank, pe_tok, k_dram, (b - 8) * 512)
        if b < 12:
            tv = kb.step("vector", lambda e, stg=stg, bank=bank:
                         e.tensor_copy(stg, kb.PS[bank][:, 0:512]),
                         waits=w, inc="ve")
            st_q[grp] = kb.dma(
                "sync", out=v_dram[tt * 128:(tt + 1) * 128,
                                   (b - 10) * 512:(b - 9) * 512],
                in_=stg, waits=[tv], sem="st")
            return tv
        if b < 20:
            tv = kb.step("scalar", lambda e, stg=stg, bank=bank:
                         e.activation(stg, kb.PS[bank][:, 0:512], AF.Silu),
                         waits=w, inc="se")
            st_q[grp] = kb.dma(
                "sync", out=sg1_dram[tt * 128:(tt + 1) * 128,
                                     (b - 12) * 512:(b - 11) * 512],
                in_=stg, waits=[tv], sem="st")
            return tv
        tv = kb.step("vector", lambda e, tt=tt, bank=bank:
                     e.tensor_copy(kb.f32(8448 + tt * 32, 32),
                                   kb.PS[bank][:, 0:32]),
                     waits=w, inc="ve")
        st_q[grp] = tv
        dt_toks.append(tv)
        return tv

    emit_proj(kb, "l1a_wcat", 32, 10752, evac=evac_l1,
              mm_waits=[hT_tok], load0_waits=[hT_tok], load1_waits=[])
    ipA = [kb.tok("st"), kb.tok("ve"), kb.tok("se")]

    dtb_in = kb.inp("l1a_dtb", [128, 32])
    al_in = kb.inp("l1a_alog", [128, 32])
    DTB2 = kb.f32(8320, 32)
    AT2 = kb.f32(8352, 32)
    ld_dtb = kb.dma("sync", out=DTB2, in_=dtb_in, sem="a")
    ld_al = kb.dma("sync", out=AT2, in_=al_in, sem="a")
    t_a1 = kb.step("scalar", lambda e: e.activation(AT2, AT2, AF.Exp),
                   waits=[ld_al], inc="se")
    t_an = kb.step("vector", lambda e: e.tensor_scalar_mul(AT2, AT2, -1.0),
                   waits=[t_a1], inc="ve")
    one_ap, one_tok = kb.ccol(1.0)
    neg1_ap, neg1_tok = kb.ccol(-1.0)
    eps_ap, eps_tok = kb.ccol(EPS)
    A2B = 72288
    sp2_toks = []
    ab2_toks = []
    for c in range(NCH):
        dtc = kb.f32(8448 + c * 32, 32)
        t0 = kb.f32(8720, 32)
        ab = kb.f32(8752, 32)
        ab2 = kb.f32(8784, 32)
        lg = kb.f32(8816, 32)
        rr = kb.f32(8848, 32)
        w0 = [ld_dtb] + dt_toks + sp2_toks[-1:]
        t1 = kb.step("vector", lambda e, dtc=dtc, t0=t0:
                     e.tensor_add(t0, dtc, DTB2), waits=w0, inc="ve")
        t2 = kb.step("scalar", lambda e, ab=ab, t0=t0:
                     e.activation(ab, t0, AF.Abs), waits=[t1], inc="se")
        t3 = kb.step("scalar", lambda e, ab=ab, ab2=ab2:
                     e.activation(ab2, ab, AF.Exp, scale=neg1_ap),
                     waits=[t2, neg1_tok], inc="se")
        t4 = kb.step("scalar", lambda e, lg=lg, ab2=ab2:
                     e.activation(lg, ab2, AF.Ln, bias=one_ap),
                     waits=[t3, one_tok], inc="se")
        t5 = kb.step("vector", lambda e, rr=rr, t0=t0:
                     e.tensor_scalar_max(rr, t0, 0.0), waits=[t1], inc="ve")
        t6 = kb.step("vector", lambda e, dtc=dtc, rr=rr, lg=lg:
                     e.tensor_add(dtc, rr, lg), waits=[t4, t5], inc="ve")
        sp2_toks.append(t6)
        ab2_toks.append(kb.step("vector", lambda e, c=c, dtc=dtc:
                        e.tensor_tensor(kb.bf(A2B + c * 32, 32), dtc, AT2,
                                        ALU.mult), waits=[t6, t_an], inc="ve"))

    QT = 0
    KT8 = 32768
    qt_toks = []
    for hh in range(32):
        qt_toks.append(kb.dma("sync", out=kb.bf(QT + hh * T, T),
                              in_=q_dram[:, hh * 128:(hh + 1) * 128],
                              waits=ipA if hh == 0 else (), sem="a",
                              transpose=True))
    for hh in range(8):
        qt_toks.append(kb.dma("sync", out=kb.bf(KT8 + hh * T, T),
                              in_=k_dram[:, hh * 128:(hh + 1) * 128],
                              waits=(), sem="a", transpose=True))
    qt_done = qt_toks[-1]

    trili_in = kb.inp("c_trili_bf", [128, 128], BF16)
    trili_f_in = kb.inp("c_trili_f32", [128, 128])
    trius_in = kb.inp("c_trius_bf", [128, 128], BF16)
    ones_in = kb.inp("c_ones_bf", [128, 128], BF16)
    nones_in = kb.inp("c_negones_bf", [128, 128], BF16)
    g1_in = kb.inp("l1a_g1", [128, 128])
    TRI = kb.bf(76288, 128)
    TRU = kb.bf(76416, 128)
    ONE = kb.bf(76544, 128)
    NON = kb.bf(76672, 128)
    TRIF = kb.f32(9472, 128)
    G1F = kb.f32(9600, 128)
    wld = [ipA[1], ipA[2]]
    ld_c1 = kb.dma("gpsimd", out=TRI, in_=trili_in, waits=wld, sem="g")
    ld_c2 = kb.dma("gpsimd", out=TRU, in_=trius_in, sem="g")
    ld_c3 = kb.dma("gpsimd", out=ONE, in_=ones_in, sem="g")
    ld_c4 = kb.dma("gpsimd", out=NON, in_=nones_in, sem="g")
    ld_c6 = kb.dma("sync", out=TRIF, in_=trili_f_in, sem="a")
    ld_g1 = kb.dma("sync", out=G1F, in_=g1_in, sem="a")

    ps0 = kb.PS[0]
    sum_mm = []
    sr_ev1 = None
    for c in range(NCH):
        m = kb.step("tensor", lambda e, c=c: e.matmul(
            ps0[0:1, (c % 4) * 32:(c % 4) * 32 + 32], ONE[:, 0:1],
            kb.bf(A2B + c * 32, 32), start=(c % 4 == 0), stop=(c % 4 == 3)),
            waits=[ab2_toks[c], ld_c3] + ([kb.tok("ve")] if c == 4 else []),
            inc="pe")
        sum_mm.append(m)
        if c == 3:
            sr_ev1 = kb.step("vector", lambda e: e.tensor_copy(
                kb.AF[0:1, 4096:4224], ps0[0:1, 0:128]), waits=[m], inc="ve")
    SR = kb.AF[0:1, 4096:4352]
    t_sr = kb.step("vector", lambda e: e.tensor_copy(
        kb.AF[0:1, 4224:4352], ps0[0:1, 0:128]),
        waits=[sum_mm[-1], sr_ev1], inc="ve")
    SUF = kb.AF[0:1, 4352:4608]
    TOT = kb.AF[0:1, 4608:4640]
    sufs = [kb.step("vector", lambda e: e.memset(kb.AF[0:1, 4352 + 7 * 32:4608],
                                                 0.0), waits=[t_sr], inc="ve")]
    for c in range(6, -1, -1):
        sufs.append(kb.step("vector", lambda e, c=c: e.tensor_add(
            kb.AF[0:1, 4352 + c * 32:4352 + (c + 1) * 32],
            kb.AF[0:1, 4352 + (c + 1) * 32:4352 + (c + 2) * 32],
            kb.AF[0:1, 4096 + (c + 1) * 32:4096 + (c + 2) * 32]),
            waits=[sufs[-1]], inc="ve"))
    SUFB2 = 72800
    t_sufb = kb.step("vector", lambda e: e.tensor_copy(
        kb.ABF[0:1, SUFB2:SUFB2 + 256], SUF), waits=[sufs[-1]], inc="ve")
    DEC2R = 72544
    t_decr = kb.step("scalar", lambda e: e.activation(
        kb.ABF[0:1, DEC2R:DEC2R + 256], SR, AF.Exp), waits=[t_sr], inc="se")
    t_tot = kb.step("vector", lambda e: e.tensor_add(
        TOT, kb.AF[0:1, 4352:4384], kb.AF[0:1, 4096:4128]),
        waits=[sufs[-1]], inc="ve")
    t_dtot = kb.step("scalar", lambda e: e.activation(TOT, TOT, AF.Exp),
                     waits=[t_tot], inc="se")

    STATE2B = 45056
    STATE2 = kb.ABF.ap().bitcast(F32)[:, STATE2B // 2: STATE2B // 2 + 4096]
    VC = [40960, 41984]
    XC = 57344
    KD = 61440
    DSC2 = 72224
    DT2b = 72256
    c1_done = []
    wv_toks = []
    for c in range(NCH):
        slot = c % 2
        w = [ipA[0]] if c < 2 else [c1_done[c - 2]]
        l_v = kb.dma("sync", out=kb.bf(VC[slot], 1024),
                     in_=v_dram[c * 128:(c + 1) * 128, :], waits=w, sem="a")
        psd = kb.PS[1]
        md = kb.step("tensor", lambda e, c=c: [
            e.matmul(psd[:, 0:32], TRU, kb.bf(A2B + c * 32, 32),
                     start=True, stop=False),
            e.matmul(psd[:, 0:32], ONE[0:1, :],
                     kb.ABF[0:1, SUFB2 + c * 32:SUFB2 + (c + 1) * 32],
                     start=False, stop=True)],
            waits=[ab2_toks[c], ld_c2, t_sufb] + c1_done[-1:], inc="pe")
        t_dsg = kb.step("scalar", lambda e: e.activation(
            kb.bf(DSC2, 32), psd[:, 0:32], AF.Exp), waits=[md], inc="se")
        t_dt2 = kb.step("vector", lambda e, c=c: e.tensor_tensor(
            kb.bf(DT2b, 32), kb.f32(8448 + c * 32, 32), kb.bf(DSC2, 32),
            ALU.mult), waits=[t_dsg, sp2_toks[c]], inc="ve")
        t_xc = kb.step("vector", lambda e, c=c, slot=slot: e.tensor_tensor(
            kb.bf(XC, 4096).rearrange("p (kv r w) -> p kv r w", kv=8, r=4),
            kb.bf(VC[slot], 1024).rearrange("p (kv w) -> p kv w", kv=8)
            .unsqueeze(2).to_broadcast([128, 8, 4, 128]),
            kb.f32(8448 + c * 32, 32).rearrange("p (kv r) -> p kv r", kv=8)
            .unsqueeze(3).to_broadcast([128, 8, 4, 128]),
            ALU.mult), waits=[l_v, sp2_toks[c]] + c1_done[-1:], inc="ve")
        l_k = kb.dma("sync", out=kb.bf(74592, 1024),
                     in_=k_dram[c * 128:(c + 1) * 128, :], waits=w, sem="a")
        t_kd = kb.step("vector", lambda e, c=c: e.tensor_tensor(
            kb.bf(KD, 4096).rearrange("p (kv r w) -> p kv r w", kv=8, r=4),
            kb.bf(74592, 1024).rearrange("p (kv w) -> p kv w", kv=8)
            .unsqueeze(2).to_broadcast([128, 8, 4, 128]),
            kb.bf(DSC2, 32).rearrange("p (kv r) -> p kv r", kv=8)
            .unsqueeze(3).to_broadcast([128, 8, 4, 128]),
            ALU.mult), waits=[l_k, t_dsg] + c1_done[-1:], inc="ve")
        for wv in range(8):
            bank = 2 + wv % 2

            def smm(e, wv=wv, bank=bank):
                r = []
                for hh in range(4):
                    h = wv * 4 + hh
                    r.append(e.matmul(
                        kb.PS[bank][:, hh * 128:hh * 128 + 128],
                        kb.bf(KD + h * 128, 128),
                        kb.bf(XC + h * 128, 128),
                        start=(hh == 0), stop=(hh == 3)))
                return r
            wsm = [t_kd, t_xc]
            if len(wv_toks) >= 2:
                wsm.append(wv_toks[-2])
            tm = kb.step("tensor", smm, waits=wsm, inc="pe")

            def sadd(e, c=c, wv=wv, bank=bank):
                dst = STATE2[:, wv * 512:wv * 512 + 512]
                if c == 0:
                    return e.tensor_copy(dst, kb.PS[bank][:, 0:512])
                return e.tensor_add(dst, dst, kb.PS[bank][:, 0:512])
            wv_toks.append(kb.step("vector", sadd, waits=[tm], inc="ve"))
        c1_done.append(wv_toks[-1])

    pay = kb.dram("l1a_pay", [129, 4096], F32)
    gath = kb.dram("l1a_gath", [8 * 129, 4096], F32)
    d_p1 = kb.dma("sync", out=pay[0:128, :], in_=STATE2,
                  waits=[c1_done[-1]], sem="st")
    d_p2 = kb.dma("sync", out=pay[128:129, 0:32], in_=TOT, waits=[t_dtot],
                  sem="st")
    ag = kb.step("gpsimd", lambda e: e.collective_compute(
        "AllGather", ALU.bypass, replica_groups=[list(range(NCORES))],
        ins=[pay.opt()], outs=[gath.opt()]),
        waits=[d_p1, d_p2], inc="cc")
    cm_in = kb.inp("l0a_cm", [1, 16])
    CM = kb.AF[0:1, 9728:9744]
    ld_cm = kb.dma("sync", out=CM, in_=cm_in, sem="a")
    CMa = lambda j: kb.AF[0:1, 9728 + j:9729 + j]
    CMb = lambda j: kb.AF[0:1, 9736 + j:9737 + j]
    MDTr = kb.AF[0:1, 4640:4672]
    PRUN = kb.AF[0:1, 4672:4704]
    WJF = kb.AF[0:1, 4704:4736]
    WJB = kb.ABF[0:1, SUFB2:SUFB2 + 32]
    SFJ = kb.ABF.ap().bitcast(F32)[:, 57344 // 2: 57344 // 2 + 4096]
    t_zero = kb.step("vector", lambda e: e.memset(STATE2, 0.0),
                     waits=[d_p1], inc="ve")
    t_p = kb.step("vector", lambda e: e.memset(PRUN, 1.0), waits=[ag],
                  inc="ve")
    acc_tok = t_zero
    ps_w = kb.PS[6]
    WBC = kb.bf(DSC2 - 64 if False else 72192, 32)
    for j in range(7, -1, -1):
        ldj = kb.dma("sync", out=MDTr,
                     in_=gath[j * 129 + 128:j * 129 + 129, 0:32],
                     waits=[ag], sem="a")
        t_m = kb.step("vector", lambda e, j=j: e.tensor_scalar(
            MDTr, MDTr, CMa(j), CMb(j), ALU.mult, ALU.add),
            waits=[ldj, ld_cm, t_p], inc="ve")
        t_wj = kb.step("vector", lambda e, j=j: e.tensor_scalar_mul(
            WJF, PRUN, CMa(j)), waits=[t_m], inc="ve")
        t_wjb = kb.step("vector", lambda e: e.tensor_copy(WJB, WJF),
                        waits=[t_wj], inc="ve")
        t_p = kb.step("vector", lambda e: e.tensor_tensor(
            PRUN, PRUN, MDTr, ALU.mult), waits=[t_wjb], inc="ve")
        m_b = kb.step("tensor", lambda e: e.matmul(
            ps_w[:, 0:32], ONE[0:1, :], WJB, start=True, stop=True),
            waits=[t_wjb] + ([] if j == 7 else [acc_tok]), inc="pe")
        t_bc = kb.step("vector", lambda e: e.tensor_copy(WBC, ps_w[:, 0:32]),
                       waits=[m_b], inc="ve")
        ldS = kb.dma("sync", out=SFJ, in_=gath[j * 129:j * 129 + 128, :],
                     waits=[ag] + ([] if j == 7 else [acc_tok]), sem="a")
        t_mu = kb.step("vector", lambda e: e.tensor_tensor(
            SFJ.rearrange("p (h w) -> p h w", w=DH),
            SFJ.rearrange("p (h w) -> p h w", w=DH),
            WBC.unsqueeze(2).to_broadcast([128, 32, DH]), ALU.mult),
            waits=[ldS, t_bc], inc="ve")
        acc_tok = kb.step("vector", lambda e: e.tensor_add(
            STATE2, STATE2, SFJ), waits=[t_mu, t_zero], inc="ve")
    sin_tok = acc_tok

    PREV2 = 53248
    RHSA2 = 65536
    SM = 69632
    LL = 70144
    LL2 = 70656
    T3Q = [71168, 71680]
    EA2 = 72192
    SGL1 = 43008
    Y1O = 73056
    YT1b = 74080
    GJF = 4096
    c2_last = []
    t3w = {}
    my_toks = {}
    tp2_toks = {}
    ts1_toks = {}
    clamp_toks = {}
    tt1_toks = {}
    sm_toks = {}
    sgl_toks = {}
    ysc_toks = {}
    yst_toks = {}
    for c in range(NCH):
        slot = c % 2
        w_ld = [sin_tok] if c < 2 else [c2_last[c - 2]]
        l_v = kb.dma("sync", out=kb.bf(VC[slot], 1024),
                     in_=v_dram[c * 128:(c + 1) * 128, :], waits=w_ld, sem="a")
        l_k = kb.dma("sync", out=kb.bf(74592, 1024),
                     in_=k_dram[c * 128:(c + 1) * 128, :],
                     waits=w_ld + c2_last[-1:], sem="a")
        t_prev = kb.step("vector", lambda e: e.tensor_copy(
            kb.bf(PREV2, 4096), STATE2),
            waits=([sin_tok] if c == 0 else [c2_last[c - 1]]), inc="ve")
        psa = kb.PS[1]
        m_acs = kb.step("tensor", lambda e, c=c: e.matmul(
            psa[:, 0:32], TRI, kb.bf(A2B + c * 32, 32),
            start=True, stop=True),
            waits=[ld_c1, ab2_toks[c]] + c2_last[-1:] +
                  ([sin_tok] if c == 0 else []), inc="pe")
        t_ea = kb.step("scalar", lambda e: e.activation(
            kb.bf(EA2, 32), psa[:, 0:32], AF.Exp), waits=[m_acs], inc="se")
        m_dsc = kb.step("tensor", lambda e, c=c: e.matmul(
            psa[:, 32:64], TRU, kb.bf(A2B + c * 32, 32),
            start=True, stop=True),
            waits=[ld_c2, ab2_toks[c], t_ea] + ([sin_tok] if c == 0 else []),
            inc="pe")
        t_dsc = kb.step("scalar", lambda e: e.activation(
            kb.bf(DSC2, 32), psa[:, 32:64], AF.Exp), waits=[m_dsc], inc="se")
        t_xc = kb.step("vector", lambda e, c=c, slot=slot: e.tensor_tensor(
            kb.bf(XC, 4096).rearrange("p (kv r w) -> p kv r w", kv=8, r=4),
            kb.bf(VC[slot], 1024).rearrange("p (kv w) -> p kv w", kv=8)
            .unsqueeze(2).to_broadcast([128, 8, 4, 128]),
            kb.f32(8448 + c * 32, 32).rearrange("p (kv r) -> p kv r", kv=8)
            .unsqueeze(3).to_broadcast([128, 8, 4, 128]),
            ALU.mult), waits=[l_v, sp2_toks[c]] + c2_last[-1:], inc="ve")
        t_kd = kb.step("vector", lambda e, c=c: e.tensor_tensor(
            kb.bf(KD, 4096).rearrange("p (kv r w) -> p kv r w", kv=8, r=4),
            kb.bf(74592, 1024).rearrange("p (kv w) -> p kv w", kv=8)
            .unsqueeze(2).to_broadcast([128, 8, 4, 128]),
            kb.bf(DSC2, 32).rearrange("p (kv r) -> p kv r", kv=8)
            .unsqueeze(3).to_broadcast([128, 8, 4, 128]),
            ALU.mult), waits=[l_k, t_dsc] + c2_last[-1:], inc="ve")
        wh = [ab2_toks[c], ld_c1]
        if c > 0:
            wh.append(c2_last[c - 1])
        rh_tok = kb.step("vector", lambda e, c=c: e.tensor_tensor(
            kb.bf(RHSA2, 4096).rearrange("p (h i) -> p h i", i=128),
            TRI.unsqueeze(1).to_broadcast([128, 32, 128]),
            kb.bf(A2B + c * 32, 32).unsqueeze(2).to_broadcast([128, 32, 128]),
            ALU.mult), waits=wh, inc="ve")
        for q4 in range(8):
            psd = kb.PS[3 + q4 % 2]
            psS = kb.PS[2]

            def smmq(e, q4=q4, c=c):
                r = []
                for hh in range(4):
                    h = q4 * 4 + hh
                    kv = h // 4
                    r.append(e.matmul(
                        psS[:, hh * 128:hh * 128 + 128],
                        kb.bf(KT8 + kv * T + c * 128, 128),
                        kb.bf(QT + h * T + c * 128, 128),
                        start=(hh == 0), stop=(hh == 3)))
                return r
            wsq = [qt_done]
            if (c, q4 - 1) in sm_toks:
                wsq.append(sm_toks[(c, q4 - 1)])
            elif c > 0 and q4 == 0:
                wsq.append(sm_toks[(c - 1, 7)])
            m_s = kb.step("tensor", smmq, waits=wsq, inc="pe")
            sm_toks[(c, q4)] = kb.step("vector", lambda e: e.tensor_tensor(
                kb.bf(SM, 512).rearrange("p (f i) -> p f i", f=4),
                psS[:, 0:512].rearrange("p (f i) -> p f i", f=4),
                TRIF.unsqueeze(1).to_broadcast([128, 4, 128]), ALU.mult),
                waits=[m_s, ld_c6] + ([t3w[(c, q4 - 1)]] if (c, q4 - 1) in t3w
                                      else ([t3w[(c - 1, 7)]] if c > 0 and q4 == 0
                                            else [])), inc="ve")

            def dmm(e, q4=q4, psd=psd):
                r = [e.matmul(psd[:, 0:512], ONE,
                              kb.bf(RHSA2 + q4 * 512, 512),
                              start=True, stop=False)]
                for hi in range(4):
                    r.append(e.matmul(
                        psd[:, hi * 128:hi * 128 + 128],
                        kb.bf(RHSA2 + q4 * 512 + hi * 128, 128), NON,
                        start=False, stop=(hi == 3)))
                return r
            wq = [rh_tok, ld_c3, ld_c4]
            if (c, q4 - 2) in clamp_toks:
                wq.append(clamp_toks[(c, q4 - 2)])
            elif c > 0 and q4 < 2:
                wq.append(clamp_toks[(c - 1, q4 + 6)])
            m_d = kb.step("tensor", dmm, waits=wq, inc="pe")
            wcl = [m_d]
            if (c, q4 - 1) in tt1_toks:
                wcl.append(tt1_toks[(c, q4 - 1)])
            elif c > 0 and q4 == 0:
                wcl.append(tt1_toks[(c - 1, 7)])
            clamp_toks[(c, q4)] = kb.step(
                "vector", lambda e, psd=psd: e.tensor_scalar_min(
                    kb.bf(LL, 512), psd[:, 0:512], 0.0), waits=wcl, inc="ve")
            wex = [clamp_toks[(c, q4)]]
            if (c, q4 - 1) in t3w:
                wex.append(t3w[(c, q4 - 1)])
            elif c > 0 and q4 == 0:
                wex.append(t3w[(c - 1, 7)])
            t_ex = kb.step("scalar", lambda e: e.activation(
                kb.bf(LL2, 512), kb.bf(LL, 512), AF.Exp),
                waits=wex, inc="se")
            tt1_toks[(c, q4)] = t_ex
            wt3 = [t_ex, sm_toks[(c, q4)]]
            if (c, q4 - 2) in my_toks:
                wt3.append(my_toks[(c, q4 - 2)])
            elif c > 0 and q4 < 2:
                wt3.append(my_toks[(c - 1, q4 + 6)])
            t3w[(c, q4)] = kb.step(
                "gpsimd", lambda e, q4=q4: e.tensor_tensor(
                    kb.bf(T3Q[q4 % 2], 512), kb.bf(LL2, 512),
                    kb.bf(SM, 512), ALU.mult), waits=wt3, inc="po")
            grp = q4
            h0 = grp * 4
            bA, bB = kb.PS[5], kb.PS[6]

            def ymm(e, h0=h0, q4=q4, c=c, bA=bA, bB=bB):
                r = []
                for hh in range(4):
                    h = h0 + hh
                    r.append(e.matmul(
                        bA[:, hh * 128:hh * 128 + 128],
                        kb.bf(T3Q[q4 % 2] + hh * 128, 128),
                        kb.bf(XC + h * 128, 128),
                        start=(hh == 0), stop=(hh == 3)))
                for hh in range(4):
                    h = h0 + hh
                    r.append(e.matmul(
                        bB[:, hh * 128:hh * 128 + 128],
                        kb.bf(QT + h * T + c * 128, 128),
                        kb.bf(PREV2 + h * 128, 128),
                        start=(hh == 0), stop=(hh == 3)))
                return r
            wy = [t3w[(c, q4)], t_xc, t_prev, qt_done]
            if (c, grp - 1) in tp2_toks:
                wy += [tp2_toks[(c, grp - 1)], ts1_toks[(c, grp - 1)]]
            elif c > 0 and grp == 0:
                wy += [tp2_toks[(c - 1, 7)], ts1_toks[(c - 1, 7)]]
            my_toks[(c, grp)] = kb.step("tensor", ymm, waits=wy, inc="pe")
            wsg = []
            if (c, grp - 2) in sgl_toks:
                wsg.append(ysc_toks.get((c, grp - 2), sgl_toks[(c, grp - 2)]))
            elif c > 0 and grp < 2:
                wsg.append(ysc_toks[(c - 1, grp + 6)])
            sgl_toks[(c, grp)] = kb.dma(
                "sync", out=kb.bf(SGL1 + (grp % 2) * 512, 512),
                in_=sg1_dram[c * 128:(c + 1) * 128, h0 * DH:h0 * DH + 512],
                waits=wsg, sem="a")
            ts1_toks[(c, grp)] = kb.step(
                "vector", lambda e, h0=h0, bB=bB: e.tensor_tensor(
                    kb.bf(YT1b, 512).rearrange("p (h w) -> p h w", w=DH),
                    bB[:, 0:512].rearrange("p (h w) -> p h w", w=DH),
                    kb.bf(EA2, 32)[:, h0:h0 + 4].unsqueeze(2)
                    .to_broadcast([128, 4, DH]), ALU.mult),
                waits=[my_toks[(c, grp)], t_ea], inc="ve")
            ygb = kb.f32((grp % 2) * 512, 512)
            wp2 = [ts1_toks[(c, grp)]]
            if (c, grp - 2) in ysc_toks:
                wp2.append(ysc_toks[(c, grp - 2)])
            elif c > 0 and grp < 2:
                wp2.append(ysc_toks[(c - 1, grp + 6)])
            tp2_toks[(c, grp)] = kb.step(
                "vector", lambda e, ygb=ygb, bA=bA: e.tensor_add(
                    ygb, kb.bf(YT1b, 512), bA[:, 0:512]),
                waits=wp2, inc="ve")
            ssq = kb.f32(8880 + (grp % 2) * 4, 4)
            t_sq = kb.step("vector", lambda e, ygb=ygb, ssq=ssq, grp=grp:
                e.scalar_tensor_tensor(
                    kb.f32(GJF, 512), ygb, 1.0, ygb, ALU.mult, ALU.mult),
                waits=[tp2_toks[(c, grp)]], inc="ve")
            red_toks = []
            for hh in range(4):
                red_toks.append(kb.step("vector", lambda e, hh=hh, ssq=ssq:
                    e.reduce_sum(ssq[:, hh:hh + 1],
                                 kb.f32(GJF + hh * 128, 128), axis=AX.X),
                    waits=[t_sq], inc="ve"))
            t_r1 = kb.step("vector", lambda e, ssq=ssq: e.tensor_scalar_mul(
                ssq, ssq, 1.0 / DH), waits=red_toks, inc="ve")
            t_r2 = kb.step("scalar", lambda e, ssq=ssq: e.activation(
                ssq, ssq, AF.Sqrt, bias=eps_ap), waits=[t_r1, eps_tok],
                inc="se")
            t_r3 = kb.step("vector", lambda e, ssq=ssq: e.reciprocal(ssq, ssq),
                           waits=[t_r2], inc="ve")
            t_n1 = kb.step("vector", lambda e, ygb=ygb, ssq=ssq: e.tensor_tensor(
                kb.f32(GJF, 512).rearrange("p (h w) -> p h w", w=DH),
                ygb.rearrange("p (h w) -> p h w", w=DH),
                ssq.unsqueeze(2).to_broadcast([128, 4, DH]), ALU.mult),
                waits=[t_r3], inc="ve")
            t_n2 = kb.step("vector", lambda e: e.tensor_tensor(
                kb.f32(GJF, 512).rearrange("p (h w) -> p h w", w=DH),
                kb.f32(GJF, 512).rearrange("p (h w) -> p h w", w=DH),
                G1F.unsqueeze(1).to_broadcast([128, 4, 128]), ALU.mult),
                waits=[t_n1, ld_g1], inc="ve")
            wn3 = [t_n2, sgl_toks[(c, grp)]]
            if (c, grp - 2) in yst_toks:
                wn3.append(yst_toks[(c, grp - 2)])
            elif c > 0 and grp < 2:
                wn3.append(yst_toks[(c - 1, grp + 6)])
            ysc_toks[(c, grp)] = kb.step(
                "gpsimd", lambda e, grp=grp: e.tensor_tensor(
                    kb.bf(Y1O + (grp % 2) * 512, 512), kb.f32(GJF, 512),
                    kb.bf(SGL1 + (grp % 2) * 512, 512), ALU.mult),
                waits=wn3, inc="po")
            yst_toks[(c, grp)] = kb.dma(
                "sync", out=y1_dram[c * 128:(c + 1) * 128,
                                    h0 * DH:h0 * DH + 512],
                in_=kb.bf(Y1O + (grp % 2) * 512, 512),
                waits=[ysc_toks[(c, grp)]], sem="st")
        ps7 = kb.PS[7]
        m_dec = kb.step("tensor", lambda e, c=c: e.matmul(
            ps7[:, 0:32], ONE[0:1, :],
            kb.ABF[0:1, DEC2R + c * 32:DEC2R + (c + 1) * 32],
            start=True, stop=True), waits=[t_decr, ld_c3] + c2_last[-1:],
            inc="pe")
        DECBC2 = 75616
        t_dbc = kb.step("vector", lambda e: e.tensor_copy(
            kb.bf(DECBC2, 32), ps7[:, 0:32]),
            waits=[m_dec], inc="ve")
        t_smul = kb.step("vector", lambda e: e.tensor_tensor(
            STATE2.rearrange("p (h w) -> p h w", w=DH),
            STATE2.rearrange("p (h w) -> p h w", w=DH),
            kb.bf(DECBC2, 32).unsqueeze(2).to_broadcast([128, 32, DH]),
            ALU.mult), waits=[t_dbc, t_prev], inc="ve")
        wv_tok = t_smul
        for wv in range(8):
            bank = 3 + wv % 2

            def smm2(e, wv=wv, bank=bank):
                r = []
                for hh in range(4):
                    h = wv * 4 + hh
                    r.append(e.matmul(
                        kb.PS[bank][:, hh * 128:hh * 128 + 128],
                        kb.bf(KD + h * 128, 128),
                        kb.bf(XC + h * 128, 128),
                        start=(hh == 0), stop=(hh == 3)))
                return r
            wsm = [t_kd, t_xc]
            if wv == 0:
                wsm += [clamp_toks[(c, 7)], clamp_toks[(c, 6)]]
            elif wv == 1:
                wsm.append(clamp_toks[(c, 7)])
            if wv >= 2:
                wsm.append(wv_tok)
            tm = kb.step("tensor", smm2, waits=wsm, inc="pe")
            wv_prev = wv_tok
            wv_tok = kb.step("vector", lambda e, wv=wv, bank=bank:
                e.tensor_add(STATE2[:, wv * 512:wv * 512 + 512],
                             STATE2[:, wv * 512:wv * 512 + 512],
                             kb.PS[bank][:, 0:512]),
                waits=[tm, t_smul, wv_prev], inc="ve")
        c2_last.append(wv_tok)

    barrier = kb.barrier_tokens()
    Y1T = 0
    y1t_toks = []
    for kt in range(32):
        y1t_toks.append(kb.dma("sync", out=kb.bf(Y1T + kt * T, T),
                               in_=y1_dram[:, kt * 128:(kt + 1) * 128],
                               waits=barrier if kt == 0 else (), sem="a",
                               transpose=True))
    y1t_done = y1t_toks[-1]

    ep_x = [kb.f32(0 + i * 512, 512) for i in range(4)]
    ep_o = [kb.f32(4096 + i * 512, 512) for i in range(4)]
    st_t = {}
    ld_t = {}

    def evac_wo(b, tt, grp, bank, pe_tok):
        slotp = grp % 4
        w = [barrier[0]] if grp < 4 else [st_t[grp - 4]]
        ld_t[grp] = kb.dma(
            "sync", out=ep_x[slotp],
            in_=xin[tt * 128:(tt + 1) * 128, b * 512:(b + 1) * 512],
            waits=w, sem="a")
        tv = kb.step("vector", lambda e, slotp=slotp, bank=bank:
                     e.tensor_add(ep_o[slotp], ep_x[slotp],
                                  kb.PS[bank][:, 0:512]),
                     waits=[pe_tok, ld_t[grp]], inc="ve")
        st_t[grp] = kb.dma(
            "sync", out=xout[tt * 128:(tt + 1) * 128, b * 512:(b + 1) * 512],
            in_=ep_o[slotp], waits=[tv], sem="st")
        return tv

    emit_proj(kb, "l1a_woT", 32, D, evac=evac_wo,
              lhsT=lambda kt, tt: kb.bf(Y1T + kt * T + tt * 128, 128),
              mm_waits=[y1t_done], load0_waits=barrier,
              load1_waits=[y1t_done])
    return kb.tok("st")


def _todo(kb, xi, xo, w):
    raise NotImplementedError


def _sublayer_list():
    mk = lambda f: (lambda kb, xi, xo, w: f(kb, xi, xo, w, first=(xi == "xin")))
    return [("l0ffn_1", mk(emit_l0ffn)),
            ("l0attn", mk(emit_l0attn)),
            ("l0ffn_2", mk(emit_l0ffn)),
            ("l1ffn_1", mk(emit_l1ffn)),
            ("l1attn", mk(emit_l1attn)),
            ("l1ffn_2", mk(emit_l1ffn))]


def run_groups(inputs, groups, dbg=None, dbg_sink=None):
    sh, percore = host_prep(inputs)
    x_cur = [percore[k]["xin"] for k in range(NCORES)]
    idx = 0
    for g in groups:
        kb = KB()
        tok = None
        for i, (name, emit) in enumerate(g):
            xi = "xin" if i == 0 else f"xi{idx + i}"
            xo = "xout" if i == len(g) - 1 else f"xi{idx + i + 1}"
            if i == 0:
                kb.inp("xin", [T, D])
            tok = emit(kb, xi, xo, [tok] if tok else [])
        idx += len(g)
        for r in (dbg or []):
            kb.dbg_reqs = getattr(kb, "dbg_reqs", [])
            kb.dbg_reqs.append(r)
        kb.debug_dump()
        nc = kb.finalize()
        in_maps = []
        for k in range(NCORES):
            m = {}
            for name in kb.inputs:
                if name == "xin":
                    m[name] = np.ascontiguousarray(x_cur[k])
                elif name in percore[k]:
                    m[name] = percore[k][name]
                else:
                    m[name] = sh[name]
            in_maps.append(m)
        res = run_bass_kernel_spmd(nc, in_maps, core_ids=list(range(NCORES)))
        if dbg_sink is not None:
            for r in res.results:
                dbg_sink.append({k: np.asarray(v).copy() for k, v in r.items()
                                 if k.startswith("dbg_")})
        x_cur = [np.asarray(r["xout"]).copy() for r in res.results]
    return np.concatenate(x_cur, axis=0).reshape(1, 8192, D)


def kernel(**inputs):
    return run_groups(inputs, [[sub] for sub in _sublayer_list()])
